# version 05
# baseline (speedup 1.0000x reference)
import numpy as np

import concourse.bass as bass
import concourse.mybir as mybir
from concourse.bass_utils import run_bass_kernel_spmd

F32 = mybir.dt.float32
BF16 = mybir.dt.bfloat16
AF = mybir.ActivationFunctionType
ALU = mybir.AluOpType

NCORES = 8
B = 16
T = 2048
H = 1024
E = 2048
S = 128
RPC = T // NCORES
L = 512
INV_SQRT_S = 1.0 / float(np.sqrt(S))
EPS = 1e-5
KC = H // 128
ET = E // 128
RPCB = RPC + 16
_PERM = np.r_[64:128, 0:64]


def _rope_tables():
    try:
        import jax
        cpu = jax.devices("cpu")[0]
        import jax.numpy as jnp
        with jax.default_device(cpu):
            pos = jnp.arange(T, dtype=jnp.float32)
            inv_freq = 10000.0 ** (jnp.arange(64, dtype=jnp.float32) / 64)
            sinusoid = pos[:, None] * inv_freq[None, :]
            return (np.asarray(jnp.sin(sinusoid)), np.asarray(jnp.cos(sinusoid)))
    except Exception:
        pos = np.arange(T, dtype=np.float32)
        inv_freq = (10000.0 ** (np.arange(64, dtype=np.float32) /
                                np.float32(64))).astype(np.float32)
        sinusoid = pos[:, None] * inv_freq[None, :]
        return np.sin(sinusoid).astype(np.float32), np.cos(sinusoid).astype(np.float32)


def _bias_matrix(w):
    n = w.shape[0]
    rel = np.arange(T)[None, :] - np.arange(T)[:, None] + (L - 1)
    idx = np.where(rel < 0, rel + n, rel)
    idx = np.clip(idx, 0, n - 1)
    return np.ascontiguousarray(w[idx].T)


class Prog:
    ENGS = ("sync", "gpsimd", "tensor", "vector", "scalar")

    def __init__(self, nc, es):
        self.nc = nc
        self.es = es
        self.ops = {e: [] for e in self.ENGS}
        self.sem = {}
        self.cnt = {}
        self.dma_sems = {}
        for name in ("cc", "pe", "dv", "ac",
                     "b_sync", "b_gpsimd", "b_tensor", "b_vector", "b_scalar"):
            self.sem[name] = es.enter_context(nc.semaphore(name))
            self.cnt[name] = 0

    def dsem(self, name, issuer):
        if name not in self.sem:
            self.sem[name] = self.es.enter_context(self.nc.semaphore(name))
            self.cnt[name] = 0
            self.dma_sems[name] = issuer
        return name

    def wait(self, eng, sem, val=None):
        v = self.cnt[sem] if val is None else val
        h = self.sem[sem]
        self.ops[eng].append(lambda e, h=h, v=v: e.wait_ge(h, v))

    def emit(self, eng, fn, inc=None, amt=1):
        if eng == "vector":
            self.wait("vector", "dv")
            h = self.sem["dv"]
            self.cnt["dv"] += 1
            self.ops[eng].append(lambda e, fn=fn, h=h: fn(e).then_inc(h, 1))
            return self.cnt["dv"]
        if inc is None:
            self.ops[eng].append(fn)
            return None
        h = self.sem[inc]
        self.cnt[inc] += amt
        self.ops[eng].append(lambda e, fn=fn, h=h, a=amt: fn(e).then_inc(h, a))
        return self.cnt[inc]

    def barrier(self):
        for name, issuer in self.dma_sems.items():
            if self.cnt[name]:
                self.wait(issuer, name)
        self.wait("gpsimd", "cc")
        self.wait("vector", "dv")
        self.wait("scalar", "ac")
        self.wait("tensor", "pe")
        for e in self.ENGS:
            h = self.sem["b_" + e]
            self.cnt["b_" + e] += 1
            self.ops[e].append(lambda eng, h=h: eng.nop().then_inc(h, 1))
        for e in self.ENGS:
            for o in self.ENGS:
                if o != e:
                    self.wait(e, "b_" + o)


def build_program():
    nc = bass.Bass("TRN2", target_bir_lowering=False)
    dt = nc.dram_tensor

    x_own = dt("x_own", [B, H, RPC], F32, kind="ExternalInput").ap()
    biasT_own = dt("biasT_own", [T, RPC], F32, kind="ExternalInput").ap()
    sinT_in = dt("sinT", [128, RPC], F32, kind="ExternalInput").ap()
    cosT_in = dt("cosT", [128, RPC], F32, kind="ExternalInput").ap()
    ident_in = dt("ident_in", [128, 128], F32, kind="ExternalInput").ap()

    wuT = dt("wuT", [H, E], F32, kind="ExternalInput").ap()
    wvT = dt("wvT", [H, E], F32, kind="ExternalInput").ap()
    wbT = dt("wbT", [H, S], F32, kind="ExternalInput").ap()
    wbTs = dt("wbTs", [H, S], F32, kind="ExternalInput").ap()
    owT = dt("owT", [E, H], F32, kind="ExternalInput").ap()
    woT = dt("woT", [H, H], F32, kind="ExternalInput").ap()
    gateT = {g: dt(g + "T", [H, H], F32, kind="ExternalInput").ap()
             for g in ("wf", "uf", "wi", "ui", "wc", "uc", "uo")}

    vec_in = {}
    for name, n in (("ln_w", H), ("ln_b", H), ("o_b", H), ("bf", H), ("bi", H),
                    ("bc", H), ("bo", H), ("in_c", H), ("uvb_u", E), ("uvb_b", S), ("uvb_bs", S)):
        vec_in[name] = dt(name, [n], F32, kind="ExternalInput").ap()
    uvb_v_in = dt("uvb_v", [E], F32, kind="ExternalInput").ap()
    gamma_in = dt("gamma", [2, 128], F32, kind="ExternalInput").ap()
    gammas_in = dt("gammas", [2, 128], F32, kind="ExternalInput").ap()
    beta_in = dt("beta", [2, 128], F32, kind="ExternalInput").ap()
    betas_in = dt("betas", [2, 128], F32, kind="ExternalInput").ap()

    y_own = dt("y_own", [B, H, RPC], F32, kind="ExternalOutput").ap()

    wu_h = dt("wu_h", [H, E], BF16)
    wv_h = dt("wv_h", [H, E], BF16)
    wb_h = dt("wb_h", [H, S], BF16)
    wbs_h = dt("wbs_h", [H, S], BF16)
    ow_h = dt("ow_h", [E, H], BF16)
    uo_h = dt("uo_h", [H, H], BF16)
    ag_in = [dt(f"ag_in{b}", [RPCB, E], BF16) for b in range(B)]
    ag_out = [dt(f"ag_out{b}", [NCORES * RPCB, E], BF16) for b in range(B)]

    from contextlib import ExitStack
    es = ExitStack()
    with es:
        e = es.enter_context

        xnT = e(nc.sbuf_tensor("xnT", [128, KC * B * RPC], BF16))
        bb = [e(nc.sbuf_tensor(f"bb{i}", [128, 8192], BF16)) for i in range(6)]
        aux_h = e(nc.sbuf_tensor("aux_h", [128, 3072], BF16))
        c_f32 = e(nc.sbuf_tensor("c_f32", [128, KC * RPC], F32))
        c_bfA = e(nc.sbuf_tensor("c_bfA", [128, KC * RPC], BF16))
        c_bfB = e(nc.sbuf_tensor("c_bfB", [128, KC * RPC], BF16))
        scr = e(nc.sbuf_tensor("scr", [128, 512], F32))
        scr2 = e(nc.sbuf_tensor("scr2", [128, 512], F32))
        gates = [e(nc.sbuf_tensor(f"gate{i}", [128, RPC], F32)) for i in range(3)]
        upd = e(nc.sbuf_tensor("upd", [128, RPC], F32))
        sq_h = e(nc.sbuf_tensor("sq_h", [128, RPC], BF16))
        tmp_h = e(nc.sbuf_tensor("tmp_h", [128, RPC], BF16))
        muB = e(nc.sbuf_tensor("muB", [128, RPC], BF16))
        rsB = e(nc.sbuf_tensor("rsB", [128, RPC], BF16))
        hT_b = e(nc.sbuf_tensor("hT_b", [128, RPC], BF16))
        xres = [e(nc.sbuf_tensor(f"xres{i}", [128, RPC], F32)) for i in range(2)]
        outs = [e(nc.sbuf_tensor(f"outs{i}", [128, RPC], F32)) for i in range(2)]
        stat = e(nc.sbuf_tensor("stat", [1, 2 * RPC], F32))
        uvbv_row = e(nc.sbuf_tensor("uvbv_row", [1, E], BF16))
        ones_col = e(nc.sbuf_tensor("ones_col", [128, 1], BF16))
        ones_row = e(nc.sbuf_tensor("ones_row", [1, 128], BF16))
        ones_row_f = e(nc.sbuf_tensor("ones_row_f", [1, 128], F32))
        ident_h = e(nc.sbuf_tensor("ident_h", [128, 128], BF16))
        sin_sb = e(nc.sbuf_tensor("sin_sb", [128, RPC], F32))
        cos_sb = e(nc.sbuf_tensor("cos_sb", [128, RPC], F32))
        sm = e(nc.sbuf_tensor("sm", [128, 96], F32))

        p_m = [e(nc.psum_tensor(f"p_m{i}", [128, RPC], F32)) for i in range(3)]
        p_v = [e(nc.psum_tensor(f"p_v{i}", [128, 512], F32)) for i in range(2)]
        p_s = [e(nc.psum_tensor(f"p_s{i}", [1, RPC], F32)) for i in range(2)]
        p_b = e(nc.psum_tensor("p_b", [128, 512], F32))

        P = Prog(nc, es)
        psum_owner = {}

        def group_begin(key):
            if key in psum_owner:
                s, v = psum_owner[key]
                P.wait("tensor", s, v)

        smcol = {}
        _c = [0]
        def alloc_sm(name, ncols):
            smcol[name] = _c[0]
            _c[0] += ncols
        for name in ("ln_w", "ln_b", "o_b", "bf", "bi", "bc", "bo", "in_c"):
            alloc_sm(name, KC)
        alloc_sm("uvb_u", ET)
        alloc_sm("uvb_b", 1)
        alloc_sm("uvb_bs", 1)
        alloc_sm("gamma", 2)
        alloc_sm("beta", 2)
        alloc_sm("gammas", 2)
        alloc_sm("betas", 2)
        alloc_sm("g0p", 1)
        alloc_sm("b0p", 1)
        alloc_sm("g0ps", 1)
        alloc_sm("b0ps", 1)
        assert _c[0] <= 96

        def smc(name, i=0):
            return sm[:, smcol[name] + i: smcol[name] + i + 1]

        WG = {g: bb[i] for i, g in enumerate(("wf", "uf", "wi", "ui", "wc", "uc"))}
        kernelT = bb[0]
        kT_sb = bb[0][:, 4096:6144]
        qT_sb = bb[0][:, 6144:6400]
        uT = bb[1][:, 0:4096]
        eT = bb[1][:, 4096:8192]
        biasT_sb = bb[2][:, 0:4096]
        uocT = bb[2][:, 4096:6144]
        xnnT = bb[2][:, 6144:8192]
        ow_strip = [bb[3][:, i * 2048:(i + 1) * 2048] for i in range(2)]
        woT_sb = bb[4]
        wu_strip = [bb[5][:, i * 1024:(i + 1) * 1024] for i in range(2)]
        wv_buf = [bb[5][:, 2048 + i * 512: 2048 + (i + 1) * 512] for i in range(2)]
        wb_buf = bb[5][:, 3072:4096]
        vstrip = [bb[5][:, 4096 + i * 2048: 4096 + (i + 1) * 2048] for i in range(2)]
        vout = [aux_h[:, i * 512:(i + 1) * 512] for i in range(2)]
        uo_strip = [aux_h[:, 1024 + i * 1024: 1024 + (i + 1) * 1024] for i in range(2)]
        x_bf = c_bfA
        uvbv_hlp = c_bfA
        cast_h = c_bfB

        def dma(eng, sem, out, in_, slow=False):
            P.dsem(sem, eng)
            if slow:
                return P.emit(eng, lambda e, o=out, i=in_: e.dma_start(
                    out=o, in_=i, allow_slow_non_contiguous=True), inc=sem, amt=16)
            return P.emit(eng, lambda e, o=out, i=in_: e.dma_start(out=o, in_=i),
                          inc=sem, amt=16)

        for name in ("ln_w", "ln_b", "o_b", "bf", "bi", "bc", "bo", "in_c"):
            dma("sync", "LS", sm[:, smcol[name]:smcol[name] + KC],
                vec_in[name].rearrange("(t p) -> p t", p=128), slow=True)
        dma("sync", "LS", sm[:, smcol["uvb_u"]:smcol["uvb_u"] + ET],
            vec_in["uvb_u"].rearrange("(t p) -> p t", p=128), slow=True)
        dma("sync", "LS", sm[:, smcol["uvb_b"]:smcol["uvb_b"] + 1],
            vec_in["uvb_b"].rearrange("(t p) -> p t", p=128), slow=True)
        dma("sync", "LS", sm[:, smcol["uvb_bs"]:smcol["uvb_bs"] + 1],
            vec_in["uvb_bs"].rearrange("(t p) -> p t", p=128), slow=True)
        dma("sync", "LS", sm[:, smcol["gamma"]:smcol["gamma"] + 2],
            gamma_in.rearrange("h s -> s h"), slow=True)
        dma("sync", "LS", sm[:, smcol["beta"]:smcol["beta"] + 2],
            beta_in.rearrange("h s -> s h"), slow=True)
        dma("sync", "LS", sm[:, smcol["gammas"]:smcol["gammas"] + 2],
            gammas_in.rearrange("h s -> s h"), slow=True)
        dma("sync", "LS", sm[:, smcol["betas"]:smcol["betas"] + 2],
            betas_in.rearrange("h s -> s h"), slow=True)
        dma("sync", "LS", scr[:, :128], ident_in)
        dma("sync", "LS", sin_sb[:, :], sinT_in)
        dma("sync", "LS", cos_sb[:, :], cosT_in)

        P.wait("vector", "LS")
        P.emit("vector", lambda e: e.memset(ones_col[:, :], 1.0))
        P.emit("vector", lambda e: e.memset(ones_row[:, :], 1.0))
        P.emit("vector", lambda e: e.memset(ones_row_f[:, :], 1.0))
        P.emit("vector", lambda e: e.tensor_copy(ident_h[:, :], scr[:, :128]))
        for q in range(4):
            dma("sync", "L1", scr2[0:1, :512],
                uvb_v_in[q * 512:(q + 1) * 512].rearrange("(a m) -> a m", a=1))
            P.wait("vector", "L1")
            dvq = P.emit("vector", lambda e, q_=q: e.tensor_copy(
                uvbv_hlp[0:1, q_ * 512:(q_ + 1) * 512], scr2[0:1, :512]), inc="dv")
            P.wait("sync", "dv", dvq)
        P.wait("gpsimd", "dv")
        dma("gpsimd", "SU", uvbv_row[0:1, :], uvbv_hlp[0:1, 0:E])
        P.emit("vector", lambda e: e.tensor_scalar_mul(smc("g0p"), smc("gamma", 0), INV_SQRT_S))
        P.emit("vector", lambda e: e.tensor_scalar_mul(smc("b0p"), smc("beta", 0), INV_SQRT_S))
        P.emit("vector", lambda e: e.tensor_scalar_mul(smc("g0ps"), smc("gammas", 0), INV_SQRT_S))
        P.emit("vector", lambda e: e.tensor_scalar_mul(smc("b0ps"), smc("betas", 0), INV_SQRT_S),
               inc="dv")

        scr_guard = [0, 0]

        def cast_chunks(chunks):
            pass

        def cast_resident(src_ap, dst, n_out):
            i = 0
            for kc in range(KC):
                for off in range(0, n_out, 512):
                    wdt = min(512, n_out - off)
                    par = i % 2
                    buf = (scr, scr2)[par]
                    if scr_guard[par]:
                        P.wait("sync", "dv", scr_guard[par])
                    dma("sync", "L%d" % par, buf[:, :wdt],
                        src_ap[kc * 128:(kc + 1) * 128, off:off + wdt])
                    P.wait("vector", "L%d" % par)
                    dvv = P.emit("vector", lambda e, b=buf[:, :wdt],
                                 d=dst[:, kc * n_out + off: kc * n_out + off + wdt]:
                                 e.tensor_copy(d, b), inc="dv")
                    scr_guard[par] = dvv
                    i += 1

        cast_dram_guard = [0, 0]

        def cast_to_dram(src_ap, dst_dram, rows, cols):
            i = 0
            for r0 in range(0, rows, 128):
                for off in range(0, cols, 512):
                    wdt = min(512, cols - off)
                    par = i % 2
                    buf = (scr, scr2)[par]
                    if scr_guard[par]:
                        P.wait("sync", "dv", scr_guard[par])
                    dma("sync", "L%d" % par, buf[:, :wdt],
                        src_ap[r0:r0 + 128, off:off + wdt])
                    hbuf = cast_h[:, par * 1024: par * 1024 + wdt]
                    P.wait("vector", "L%d" % par)
                    if cast_dram_guard[par]:
                        P.wait("vector", "SC%d" % par)
                    dvv = P.emit("vector", lambda e, b=buf[:, :wdt], h=hbuf:
                                 e.tensor_copy(h, b), inc="dv")
                    scr_guard[par] = dvv
                    P.wait("gpsimd", "dv", dvv)
                    dma("gpsimd", "SC%d" % par, dst_dram.ap()[r0:r0 + 128, off:off + wdt], hbuf)
                    cast_dram_guard[par] = 1
                    i += 1

        for g in ("wf", "uf", "wi", "ui", "wc", "uc"):
            cast_resident(gateT[g], WG[g], H)
        cast_to_dram(gateT["uo"], uo_h, H, H)
        cast_to_dram(wuT, wu_h, H, E)
        cast_to_dram(wvT, wv_h, H, E)
        cast_to_dram(wbT, wb_h, H, S)
        cast_to_dram(wbTs, wbs_h, H, S)
        cast_to_dram(owT, ow_h, E, H)

        P.barrier()

        st_mu = stat[0:1, 0:RPC]
        st_b = stat[0:1, RPC:2 * RPC]

        def ln_stats(src_slices):
            P.wait("tensor", "dv")
            P.wait("scalar", "dv")
            for kc in range(KC):
                P.emit("tensor", lambda e, s=src_slices[kc], k=kc:
                       e.matmul(p_s[0][:, :], ones_col[:, :], s,
                                start=(k == 0), stop=(k == KC - 1)),
                       inc="pe")
            for kc in range(KC):
                sq = (sq_h, tmp_h)[kc % 2]
                if kc >= 2:
                    P.wait("scalar", "pe", pe_sq[kc - 2])
                av = P.emit("scalar", lambda e, s=src_slices[kc], q=sq:
                            e.activation(out=q[:, :], in_=s, func=AF.Square), inc="ac")
                P.wait("tensor", "ac", av)
                pv = P.emit("tensor", lambda e, k=kc, q=sq:
                            e.matmul(p_s[1][:, :], ones_col[:, :], q[:, :],
                                     start=(k == 0), stop=(k == KC - 1)),
                            inc="pe")
                if kc == 0:
                    pe_sq = {}
                pe_sq[kc] = pv
            P.wait("vector", "pe")
            P.emit("vector", lambda e: e.tensor_scalar_mul(st_mu, p_s[0][0:1, :], 1.0 / H))
            P.emit("vector", lambda e: e.tensor_scalar_mul(st_b, p_s[1][0:1, :], 1.0 / H))
            P.emit("vector", lambda e: e.tensor_mul(p_s[0][0:1, :], st_mu, st_mu))
            P.emit("vector", lambda e: e.tensor_sub(st_b, st_b, p_s[0][0:1, :]))
            P.emit("vector", lambda e: e.tensor_scalar_add(st_b, st_b, EPS), inc="dv")
            P.wait("scalar", "dv")
            P.emit("scalar", lambda e: e.activation(out=st_b, in_=st_b, func=AF.Sqrt),
                   inc="ac")
            P.wait("vector", "ac")
            P.emit("vector", lambda e: e.reciprocal(st_b, st_b), inc="dv")
            P.wait("tensor", "dv")
            P.emit("tensor", lambda e: e.matmul(p_b[:, 0:RPC], ones_row_f[:, :], st_mu,
                                                start=True, stop=True))
            P.emit("tensor", lambda e: e.matmul(p_b[:, RPC:2 * RPC], ones_row_f[:, :], st_b,
                                                start=True, stop=True), inc="pe")
            P.wait("vector", "pe")
            P.emit("vector", lambda e: e.tensor_copy(muB[:, :], p_b[:, 0:RPC]))
            P.emit("vector", lambda e: e.tensor_copy(rsB[:, :], p_b[:, RPC:2 * RPC]),
                   inc="dv")

        def xnT_sl(kc, t):
            off = kc * (B * RPC) + t * RPC
            return xnT[:, off: off + RPC]

        for t in range(B):
            P.wait("vector", "pe")
            for kc in range(KC):
                par = kc % 2
                buf = (scr, scr2)[par]
                if scr_guard[par]:
                    P.wait("sync", "dv", scr_guard[par])
                dma("sync", "L%d" % par, buf[:, :RPC],
                    x_own[t, kc * 128:(kc + 1) * 128, :])
                P.wait("vector", "L%d" % par)
                dvv = P.emit("vector", lambda e, b=buf[:, :RPC],
                             d=x_bf[:, kc * RPC:(kc + 1) * RPC]:
                             e.tensor_copy(d, b), inc="dv")
                scr_guard[par] = dvv
            xs = [x_bf[:, kc * RPC:(kc + 1) * RPC] for kc in range(KC)]
            ln_stats(xs)
            for kc in range(KC):
                P.emit("vector", lambda e, s=xs[kc]: e.tensor_sub(tmp_h[:, :], s, muB[:, :]))
                P.emit("vector", lambda e, d=xnT_sl(kc, t):
                       e.tensor_mul(d, tmp_h[:, :], rsB[:, :]), inc="dv")

        P.barrier()

        for mt in range(KC):
            P.emit("vector", lambda e, m=mt: e.tensor_scalar(
                out=c_f32[:, m * RPC:(m + 1) * RPC], in0=muB[:, :],
                scalar1=0.0, scalar2=smc("in_c", m), op0=ALU.mult, op1=ALU.add))
        for mt in range(KC):
            P.emit("vector", lambda e, m=mt: e.tensor_copy(
                c_bfA[:, m * RPC:(m + 1) * RPC], c_f32[:, m * RPC:(m + 1) * RPC]),
                inc="dv")

        gatespec = [("wf", "uf", "bf", AF.Sigmoid, 0),
                    ("wi", "ui", "bi", AF.Sigmoid, 1),
                    ("wc", "uc", "bc", AF.Tanh, 2)]
        dv_after_update = P.cnt["dv"]
        for t in range(B):
            cur = c_bfA if t % 2 == 0 else c_bfB
            nxt = c_bfB if t % 2 == 0 else c_bfA
            first_group = True
            for mt in range(KC):
                for (wg, ug, bg, func, gi) in gatespec:
                    key = "m%d" % ((mt * 3 + gi) % 3)
                    pm = p_m[(mt * 3 + gi) % 3]
                    group_begin(key)
                    for kc in range(KC):
                        P.emit("tensor", lambda e, p=pm, w=WG[wg], m=mt, k=kc, t_=t:
                               e.matmul(p[:, :],
                                        w[:, k * H + m * 128: k * H + m * 128 + 128],
                                        xnT_sl(k, t_), start=(k == 0), stop=False))
                    if first_group:
                        P.wait("tensor", "dv", dv_after_update)
                        first_group = False
                    for kc in range(KC):
                        P.emit("tensor", lambda e, p=pm, u=WG[ug], m=mt, k=kc, c=cur:
                               e.matmul(p[:, :],
                                        u[:, k * H + m * 128: k * H + m * 128 + 128],
                                        c[:, k * RPC:(k + 1) * RPC],
                                        start=False, stop=(k == KC - 1)),
                               inc="pe" if kc == KC - 1 else None)
                    pv = P.cnt["pe"]
                    P.wait("scalar", "pe", pv)
                    P.wait("scalar", "dv", dv_after_update if mt == 0 else dv_mt)
                    av = P.emit("scalar", lambda e, p=pm, f=func, b=bg, m=mt, g=gi:
                                e.activation(out=gates[g][:, :], in_=p[:, :], func=f,
                                             bias=smc(b, m)), inc="ac")
                    psum_owner[key] = ("ac", av)
                P.wait("vector", "ac")
                P.emit("vector", lambda e, m=mt: e.tensor_mul(
                    upd[:, :], gates[0][:, :], c_f32[:, m * RPC:(m + 1) * RPC]))
                P.emit("vector", lambda e: e.tensor_mul(
                    gates[2][:, :], gates[1][:, :], gates[2][:, :]))
                P.emit("vector", lambda e, m=mt: e.tensor_add(
                    c_f32[:, m * RPC:(m + 1) * RPC], upd[:, :], gates[2][:, :]))
                dv_mt = P.emit("vector", lambda e, m=mt, n=nxt: e.tensor_copy(
                    n[:, m * RPC:(m + 1) * RPC], c_f32[:, m * RPC:(m + 1) * RPC]),
                    inc="dv")
            dv_after_update = P.cnt["dv"]

        c_fin = c_bfA if B % 2 == 0 else c_bfB

        P.barrier()

        uoc_pe = {}
        for ot in range(KC):
            ustr = uo_strip[ot % 2]
            if ot >= 2:
                P.wait("sync", "pe", uoc_pe[ot - 2])
            for mc in range(KC):
                dma("sync", "LR%d" % (ot % 2), ustr[:, mc * 128:(mc + 1) * 128],
                    uo_h.ap()[mc * 128:(mc + 1) * 128, ot * 128:(ot + 1) * 128])
            key = "m%d" % (ot % 2)
            pm = p_m[ot % 2]
            group_begin(key)
            P.wait("tensor", "LR%d" % (ot % 2))
            for mc in range(KC):
                P.emit("tensor", lambda e, p=pm, s=ustr, m=mc:
                       e.matmul(p[:, :], s[:, m * 128:(m + 1) * 128],
                                c_fin[:, m * RPC:(m + 1) * RPC],
                                start=(m == 0), stop=(m == KC - 1)),
                       inc="pe" if mc == KC - 1 else None)
            pv = P.cnt["pe"]
            uoc_pe[ot] = pv
            P.wait("vector", "pe", pv)
            dvv = P.emit("vector", lambda e, o=ot, p=pm: e.tensor_copy(
                uocT[:, o * RPC:(o + 1) * RPC], p[:, :]), inc="dv")
            psum_owner[key] = ("dv", dvv)

        P.barrier()

        for jt in range(ET):
            par = jt % 2
            buf = (scr, scr2)[par]
            if scr_guard[par]:
                P.wait("sync", "dv", scr_guard[par])
            dma("sync", "L%d" % par, buf[:, :RPC],
                biasT_own[jt * 128:(jt + 1) * 128, :])
            P.wait("vector", "L%d" % par)
            dvv = P.emit("vector", lambda e, b=buf[:, :RPC], j=jt:
                         e.tensor_copy(biasT_sb[:, j * RPC:(j + 1) * RPC], b), inc="dv")
            scr_guard[par] = dvv
        for kc in range(KC):
            for hh in range(2):
                par = (kc * 2 + hh) % 2
                buf = (scr, scr2)[par]
                if scr_guard[par]:
                    P.wait("sync", "dv", scr_guard[par])
                dma("sync", "L%d" % par, buf[:, :],
                    woT[kc * 128:(kc + 1) * 128, hh * 512:(hh + 1) * 512])
                P.wait("vector", "L%d" % par)
                dvv = P.emit("vector", lambda e, b=buf[:, :], k=kc, h_=hh:
                             e.tensor_copy(woT_sb[:, k * H + h_ * 512:
                                                  k * H + h_ * 512 + 512], b),
                             inc="dv")
                scr_guard[par] = dvv

        P.barrier()

        def xnn_make(b):
            xs = [xnT_sl(kc, b) for kc in range(KC)]
            ln_stats(xs)
            for kc in range(KC):
                P.emit("vector", lambda e, s=xs[kc]: e.tensor_sub(tmp_h[:, :], s, muB[:, :]))
                P.emit("vector", lambda e: e.tensor_mul(sq_h[:, :], tmp_h[:, :], rsB[:, :]))
                P.emit("vector", lambda e, k=kc: e.tensor_scalar(
                    out=xnnT[:, k * RPC:(k + 1) * RPC], in0=sq_h[:, :],
                    scalar1=smc("ln_w", k), scalar2=smc("ln_b", k),
                    op0=ALU.mult, op1=ALU.add), inc="dv")

        base_sw = c_f32[:, 0:RPC]

        def rope_make(dst, gcol, bcol, gscol, bscol, a0, a1):
            P.emit("vector", lambda e, g=gcol, b_=bcol: e.tensor_scalar(
                out=upd[:, :], in0=gates[2][:, :], scalar1=g, scalar2=b_,
                op0=ALU.mult, op1=ALU.add))
            P.emit("vector", lambda e, g=gscol, b_=bscol: e.tensor_scalar(
                out=a0, in0=base_sw, scalar1=g, scalar2=b_,
                op0=ALU.mult, op1=ALU.add))
            P.emit("vector", lambda e: e.tensor_mul(a1, a0, sin_sb[:, :]))
            P.emit("vector", lambda e: e.tensor_mul(upd[:, :], upd[:, :], cos_sb[:, :]))
            return P.emit("vector", lambda e, d=dst:
                          e.tensor_add(d, upd[:, :], a1), inc="dv")

        vout_guard = [0, 0]
        out_guard = [0, 0]

        def produce(b):
            xnn_make(b)
            dv_xnn = P.cnt["dv"]
            for et in range(ET):
                sb = wu_strip[et % 2]
                if et >= 2:
                    P.wait("sync", "pe", u_pe[et - 2])
                for kc in range(KC):
                    dma("sync", "LU%d" % (et % 2), sb[:, kc * 128:(kc + 1) * 128],
                        wu_h.ap()[kc * 128:(kc + 1) * 128, et * 128:(et + 1) * 128])
                key = "m%d" % (et % 2)
                pm = p_m[et % 2]
                group_begin(key)
                P.wait("tensor", "LU%d" % (et % 2))
                if et == 0:
                    P.wait("tensor", "dv", dv_xnn)
                for kc in range(KC):
                    pv = P.emit("tensor", lambda e, p=pm, s=sb, k=kc:
                                e.matmul(p[:, :], s[:, k * 128:(k + 1) * 128],
                                         xnnT[:, k * RPC:(k + 1) * RPC],
                                         start=(k == 0), stop=(k == KC - 1)),
                                inc="pe")
                if et == 0:
                    u_pe = {}
                u_pe[et] = pv
                P.wait("scalar", "pe", pv)
                av = P.emit("scalar", lambda e, p=pm, t_=et: e.activation(
                    out=uT[:, t_ * RPC:(t_ + 1) * RPC], in_=p[:, :], func=AF.Silu,
                    bias=smc("uvb_u", t_)), inc="ac")
                psum_owner[key] = ("ac", av)
            for swp, w_src, dst_sb, bias_name in (
                    (0, wb_h, gates[2][:, :], "uvb_b"),
                    (1, None, base_sw, "uvb_bs")):
                for kc in range(KC):
                    src = (wb_h if swp == 0 else wbs_h).ap()
                    dma("sync", "LB", wb_buf[:, kc * 128:(kc + 1) * 128],
                        src[kc * 128:(kc + 1) * 128, :])
                group_begin("m2")
                P.wait("tensor", "LB")
                for kc in range(KC):
                    P.emit("tensor", lambda e, k=kc:
                           e.matmul(p_m[2][:, :], wb_buf[:, k * 128:(k + 1) * 128],
                                    xnnT[:, k * RPC:(k + 1) * RPC],
                                    start=(k == 0), stop=(k == KC - 1)),
                           inc="pe" if kc == KC - 1 else None)
                pv = P.cnt["pe"]
                P.wait("sync", "pe", pv)
                P.wait("scalar", "pe", pv)
                av = P.emit("scalar", lambda e, d=dst_sb, bn=bias_name: e.activation(
                    out=d, in_=p_m[2][:, :], func=AF.Silu,
                    bias=smc(bn)), inc="ac")
                psum_owner["m2"] = ("ac", av)
            P.wait("vector", "ac", av)
            a0 = gates[0][:, :]
            a1 = gates[1][:, :]
            rope_make(qT_sb[:, :], smc("g0p"), smc("b0p"),
                      smc("g0ps"), smc("b0ps"), a0, a1)
            kv = rope_make(tmp_h[:, :], smc("gamma", 1), smc("beta", 1),
                           smc("gammas", 1), smc("betas", 1), a0, a1)
            P.wait("gpsimd", "dv", kv)
            dma("gpsimd", "SK",
                ag_in[b].ap()[RPC:RPC + 16, :].rearrange("a (c i) -> (a c) i", i=RPC),
                tmp_h[:, :])
            for nch in range(4):
                for kc in range(KC):
                    par = (nch * KC + kc) % 2
                    wb2 = wv_buf[par]
                    if nch * KC + kc >= 2:
                        P.wait("sync", "pe", v_pe[nch * KC + kc - 2])
                    dma("sync", "LV%d" % par, wb2[:, :],
                        wv_h.ap()[kc * 128:(kc + 1) * 128, nch * 512:(nch + 1) * 512])
                    for rt in range(2):
                        key = "v%d" % rt
                        if kc == 0:
                            group_begin(key)
                            P.emit("tensor", lambda e, p=p_v[rt], n=nch: e.matmul(
                                p[:, :], ones_row[:, :],
                                uvbv_row[0:1, n * 512:(n + 1) * 512],
                                start=True, stop=False))
                        P.wait("tensor", "LV%d" % par)
                        pv = P.emit("tensor", lambda e, p=p_v[rt], w=wb2, r=rt, k=kc:
                                    e.matmul(p[:, :],
                                             xnnT[:, k * RPC + r * 128: k * RPC + r * 128 + 128],
                                             w[:, :], start=False, stop=(k == KC - 1)),
                                    inc="pe")
                        if nch == 0 and kc == 0 and rt == 0:
                            v_pe = {}
                        v_pe[nch * KC + kc] = pv
                        if kc == KC - 1:
                            vo = vout[rt]
                            P.wait("scalar", "pe", pv)
                            if vout_guard[rt]:
                                P.wait("scalar", "SV%d" % rt)
                            av = P.emit("scalar", lambda e, p=p_v[rt], o=vo: e.activation(
                                out=o, in_=p[:, :], func=AF.Silu), inc="ac")
                            psum_owner[key] = ("ac", av)
                            P.wait("gpsimd", "ac", av)
                            dma("gpsimd", "SV%d" % rt,
                                ag_in[b].ap()[rt * 128:(rt + 1) * 128,
                                              nch * 512:(nch + 1) * 512], vo)
                            vout_guard[rt] = 1
            P.wait("gpsimd", "SV0")
            P.wait("gpsimd", "SV1")
            P.wait("gpsimd", "SK")
            P.emit("gpsimd", lambda e, b_=b: e.collective_compute(
                "AllGather", ALU.bypass, replica_groups=[list(range(NCORES))],
                ins=[ag_in[b_].ap().opt()], outs=[ag_out[b_].ap().opt()]), inc="cc")

        def consume(b):
            P.wait("gpsimd", "cc")
            for r in range(NCORES):
                blk = ag_out[b].ap()[r * RPCB + RPC: r * RPCB + RPC + 16, :]
                dma("gpsimd", "LK", kT_sb[:, r * RPC:(r + 1) * RPC],
                    blk.rearrange("a (c i) -> (a c) i", i=RPC))
            for jt in range(ET):
                key = "m%d" % (jt % 2)
                pm = p_m[jt % 2]
                group_begin(key)
                if jt == 0:
                    P.wait("tensor", "LK")
                P.emit("tensor", lambda e, p=pm, j=jt: e.matmul(
                    p[:, :], ident_h[:, :], biasT_sb[:, j * RPC:(j + 1) * RPC],
                    start=True, stop=False))
                pv = P.emit("tensor", lambda e, p=pm, j=jt: e.matmul(
                    p[:, :], kT_sb[:, j * 128:(j + 1) * 128], qT_sb,
                    start=False, stop=True), inc="pe")
                P.wait("scalar", "pe", pv)
                if jt > 0:
                    P.wait("scalar", "dv", sq_dv)
                av = P.emit("scalar", lambda e, p=pm: e.activation(
                    out=upd[:, :], in_=p[:, :], func=AF.Relu), inc="ac")
                psum_owner[key] = ("ac", av)
                P.wait("vector", "ac", av)
                sq_dv = P.emit("vector", lambda e, j=jt: e.tensor_mul(
                    kernelT[:, j * RPC:(j + 1) * RPC], upd[:, :], upd[:, :]), inc="dv")
            dv_kernel = P.cnt["dv"]
            for et in range(ET):
                vs = vstrip[et % 2]
                if et >= 2:
                    P.wait("gpsimd", "pe", kv_pe[et - 2])
                for jt in range(ET):
                    blk = ag_out[b].ap()[(jt // 2) * RPCB + (jt % 2) * 128:
                                         (jt // 2) * RPCB + (jt % 2) * 128 + 128,
                                         et * 128:(et + 1) * 128]
                    dma("gpsimd", "LW%d" % (et % 2), vs[:, jt * 128:(jt + 1) * 128], blk)
                key = "m%d" % (et % 2)
                pm = p_m[et % 2]
                group_begin(key)
                P.wait("tensor", "LW%d" % (et % 2))
                if et == 0:
                    P.wait("tensor", "dv", dv_kernel)
                for jt in range(ET):
                    pv = P.emit("tensor", lambda e, p=pm, v_=vs, j=jt: e.matmul(
                        p[:, :], v_[:, j * 128:(j + 1) * 128],
                        kernelT[:, j * RPC:(j + 1) * RPC],
                        start=(j == 0), stop=(j == ET - 1)),
                        inc="pe" if jt == ET - 1 else None)
                if et == 0:
                    kv_pe = {}
                kv_pe[et] = pv
                P.wait("vector", "pe", pv)
                dvv = P.emit("vector", lambda e, p=pm, t_=et: e.tensor_mul(
                    eT[:, t_ * RPC:(t_ + 1) * RPC], p[:, :],
                    uT[:, t_ * RPC:(t_ + 1) * RPC]), inc="dv")
                psum_owner[key] = ("dv", dvv)
            dv_eT = P.cnt["dv"]
            for ft in range(KC):
                osb = ow_strip[ft % 2]
                if ft >= 2:
                    P.wait("sync", "pe", o_pe[ft - 2])
                for ec in range(ET):
                    dma("sync", "LO%d" % (ft % 2), osb[:, ec * 128:(ec + 1) * 128],
                        ow_h.ap()[ec * 128:(ec + 1) * 128, ft * 128:(ft + 1) * 128])
                group_begin("m2")
                P.emit("tensor", lambda e, f=ft: e.matmul(
                    p_m[2][:, :], ident_h[:, :], uocT[:, f * RPC:(f + 1) * RPC],
                    start=True, stop=False))
                for kc in range(KC):
                    P.emit("tensor", lambda e, f=ft, k=kc, b_=b: e.matmul(
                        p_m[2][:, :], woT_sb[:, k * H + f * 128: k * H + f * 128 + 128],
                        xnT_sl(k, b_), start=False, stop=(k == KC - 1)),
                        inc="pe" if kc == KC - 1 else None)
                pv_h = P.cnt["pe"]
                P.wait("scalar", "pe", pv_h)
                if ft > 0:
                    P.wait("scalar", "dv", comb_dv)
                av = P.emit("scalar", lambda e, f=ft: e.activation(
                    out=hT_b[:, :], in_=p_m[2][:, :], func=AF.Tanh, bias=smc("bo", f)),
                    inc="ac")
                psum_owner["m2"] = ("ac", av)
                key = "m%d" % (ft % 2)
                pm = p_m[ft % 2]
                group_begin(key)
                P.emit("tensor", lambda e, p=pm, f=ft, b_=b: e.matmul(
                    p[:, :], ident_h[:, :], xnT_sl(f, b_), start=True, stop=False))
                P.wait("tensor", "LO%d" % (ft % 2))
                if ft == 0:
                    P.wait("tensor", "dv", dv_eT)
                for ec in range(ET):
                    P.emit("tensor", lambda e, p=pm, s=osb, c_=ec: e.matmul(
                        p[:, :], s[:, c_ * 128:(c_ + 1) * 128],
                        eT[:, c_ * RPC:(c_ + 1) * RPC],
                        start=False, stop=(c_ == ET - 1)),
                        inc="pe" if ec == ET - 1 else None)
                pv_o = P.cnt["pe"]
                if ft == 0:
                    o_pe = {}
                o_pe[ft] = pv_o
                xr = xres[ft % 2]
                dma("sync", "LX%d" % (ft % 2), xr[:, :],
                    x_own[b, ft * 128:(ft + 1) * 128, :])
                ob = outs[ft % 2]
                P.wait("vector", "pe", pv_o)
                P.wait("vector", "ac", av)
                if out_guard[ft % 2]:
                    P.wait("vector", "SO%d" % (ft % 2))
                P.emit("vector", lambda e, p=pm, f=ft, o=ob: e.scalar_tensor_tensor(
                    out=o[:, :], in0=p[:, :], scalar=smc("o_b", f), in1=hT_b[:, :],
                    op0=ALU.add, op1=ALU.add), inc="dv")
                psum_owner[key] = ("dv", P.cnt["dv"])
                P.wait("vector", "LX%d" % (ft % 2))
                comb_dv = P.emit("vector", lambda e, o=ob, x=xr: e.tensor_add(
                    o[:, :], o[:, :], x[:, :]), inc="dv")
                P.wait("gpsimd", "dv", comb_dv)
                dma("gpsimd", "SO%d" % (ft % 2), y_own[b, ft * 128:(ft + 1) * 128, :],
                    ob[:, :])
                out_guard[ft % 2] = 1
                P.wait("sync", "dv", comb_dv)

        for b in range(B):
            produce(b)
            consume(b)
            P.barrier()

        P.wait("gpsimd", "SO0")
        P.wait("gpsimd", "SO1")

        block = e(nc.Block())

        @block.sync
        def _(sync):
            for fn in P.ops["sync"]:
                fn(sync)

        @block.gpsimd
        def _(gpsimd):
            for fn in P.ops["gpsimd"]:
                fn(gpsimd)

        @block.tensor
        def _(tensor):
            for fn in P.ops["tensor"]:
                fn(tensor)

        @block.vector
        def _(vector):
            for fn in P.ops["vector"]:
                fn(vector)

        @block.scalar
        def _(scalar):
            for fn in P.ops["scalar"]:
                fn(scalar)

    return nc


_NC_CACHE = None


def kernel(**inputs):
    global _NC_CACHE
    x = np.asarray(inputs["L0_attn_x"], dtype=np.float32)
    sin_t, cos_t = _rope_tables()
    biasT = _bias_matrix(np.asarray(inputs["L0_attn_w"], np.float32))
    uv_w = np.asarray(inputs["L0_attn_uv_w"], np.float32)
    uv_b = np.asarray(inputs["L0_attn_uv_b"], np.float32)

    common = dict(
        ident_in=np.eye(128, dtype=np.float32),
        wuT=np.ascontiguousarray(uv_w[:E].T),
        wvT=np.ascontiguousarray(uv_w[E:2 * E].T),
        wbT=np.ascontiguousarray(uv_w[2 * E:].T),
        wbTs=np.ascontiguousarray(uv_w[2 * E:].T[:, _PERM]),
        owT=np.ascontiguousarray(np.asarray(inputs["L0_attn_o_w"], np.float32).T),
        woT=np.ascontiguousarray(np.asarray(inputs["L0_ffn_wo"], np.float32).T),
        wfT=np.ascontiguousarray(np.asarray(inputs["L0_ffn_wf"], np.float32).T),
        ufT=np.ascontiguousarray(np.asarray(inputs["L0_ffn_uf"], np.float32).T),
        wiT=np.ascontiguousarray(np.asarray(inputs["L0_ffn_wi"], np.float32).T),
        uiT=np.ascontiguousarray(np.asarray(inputs["L0_ffn_ui"], np.float32).T),
        wcT=np.ascontiguousarray(np.asarray(inputs["L0_ffn_wc"], np.float32).T),
        ucT=np.ascontiguousarray(np.asarray(inputs["L0_ffn_uc"], np.float32).T),
        uoT=np.ascontiguousarray(np.asarray(inputs["L0_ffn_uo"], np.float32).T),
        ln_w=np.asarray(inputs["L0_attn_ln_w"], np.float32),
        ln_b=np.asarray(inputs["L0_attn_ln_b"], np.float32),
        o_b=np.asarray(inputs["L0_attn_o_b"], np.float32),
        bf=np.asarray(inputs["L0_ffn_bf"], np.float32),
        bi=np.asarray(inputs["L0_ffn_bi"], np.float32),
        bc=np.asarray(inputs["L0_ffn_bc"], np.float32),
        bo=np.asarray(inputs["L0_ffn_bo"], np.float32),
        in_c=np.asarray(inputs["L0_ffn_in_c"], np.float32),
        uvb_u=np.ascontiguousarray(uv_b[:E]),
        uvb_v=np.ascontiguousarray(uv_b[E:2 * E]),
        uvb_b=np.ascontiguousarray(uv_b[2 * E:]),
        uvb_bs=np.ascontiguousarray(uv_b[2 * E:][_PERM]),
        gamma=np.asarray(inputs["L0_attn_gamma"], np.float32),
        beta=np.asarray(inputs["L0_attn_beta"], np.float32),
        gammas=np.ascontiguousarray(np.asarray(inputs["L0_attn_gamma"], np.float32)[:, _PERM]),
        betas=np.ascontiguousarray(np.asarray(inputs["L0_attn_beta"], np.float32)[:, _PERM]),
    )

    xT = np.ascontiguousarray(x.transpose(0, 2, 1))
    in_maps = []
    for c in range(NCORES):
        r0, r1 = c * RPC, (c + 1) * RPC
        m = dict(common)
        m["x_own"] = np.ascontiguousarray(xT[:, :, r0:r1])
        m["biasT_own"] = np.ascontiguousarray(biasT[:, r0:r1])
        st = sin_t[r0:r1].T
        ct = cos_t[r0:r1].T
        m["sinT"] = np.ascontiguousarray(np.concatenate([-st, st], axis=0))
        m["cosT"] = np.ascontiguousarray(np.concatenate([ct, ct], axis=0))
        in_maps.append(m)

    if _NC_CACHE is None:
        _NC_CACHE = build_program()
    res = run_bass_kernel_spmd(_NC_CACHE, in_maps, core_ids=list(range(NCORES)))
    shards = [res.results[c]["y_own"] for c in range(NCORES)]
    outT = np.concatenate(shards, axis=2)
    return np.ascontiguousarray(outT.transpose(0, 2, 1)).astype(np.float32)


# version 06
# speedup vs baseline: 1.3530x; 1.3530x over previous
import numpy as np

import concourse.bass as bass
import concourse.mybir as mybir
from concourse.bass_utils import run_bass_kernel_spmd

F32 = mybir.dt.float32
BF16 = mybir.dt.bfloat16
AF = mybir.ActivationFunctionType
ALU = mybir.AluOpType

NCORES = 8
B = 16
T = 2048
H = 1024
E = 2048
S = 128
RPC = T // NCORES
L = 512
INV_SQRT_S = 1.0 / float(np.sqrt(S))
EPS = 1e-5
KC = H // 128
ET = E // 128
RPCB = RPC + 16
_PERM = np.r_[64:128, 0:64]


def _rope_tables():
    try:
        import jax
        cpu = jax.devices("cpu")[0]
        import jax.numpy as jnp
        with jax.default_device(cpu):
            pos = jnp.arange(T, dtype=jnp.float32)
            inv_freq = 10000.0 ** (jnp.arange(64, dtype=jnp.float32) / 64)
            sinusoid = pos[:, None] * inv_freq[None, :]
            return (np.asarray(jnp.sin(sinusoid)), np.asarray(jnp.cos(sinusoid)))
    except Exception:
        pos = np.arange(T, dtype=np.float32)
        inv_freq = (10000.0 ** (np.arange(64, dtype=np.float32) /
                                np.float32(64))).astype(np.float32)
        sinusoid = pos[:, None] * inv_freq[None, :]
        return np.sin(sinusoid).astype(np.float32), np.cos(sinusoid).astype(np.float32)


def _bias_matrix(w):
    n = w.shape[0]
    rel = np.arange(T)[None, :] - np.arange(T)[:, None] + (L - 1)
    idx = np.where(rel < 0, rel + n, rel)
    idx = np.clip(idx, 0, n - 1)
    return np.ascontiguousarray(w[idx].T)


class Prog:
    ENGS = ("sync", "gpsimd", "tensor", "vector", "scalar")

    def __init__(self, nc, es):
        self.nc = nc
        self.es = es
        self.ops = {e: [] for e in self.ENGS}
        self.sem = {}
        self.cnt = {}
        self.dma_sems = {}
        for name in ("cc", "pe", "dv", "ac",
                     "b_sync", "b_gpsimd", "b_tensor", "b_vector", "b_scalar"):
            self.sem[name] = es.enter_context(nc.semaphore(name))
            self.cnt[name] = 0

    def dsem(self, name, issuer):
        if name not in self.sem:
            self.sem[name] = self.es.enter_context(self.nc.semaphore(name))
            self.cnt[name] = 0
            self.dma_sems[name] = issuer
        return name

    def wait(self, eng, sem, val=None):
        v = self.cnt[sem] if val is None else val
        h = self.sem[sem]
        self.ops[eng].append(lambda e, h=h, v=v: e.wait_ge(h, v))

    def emit(self, eng, fn, inc=None, amt=1):
        if eng == "vector":
            self.wait("vector", "dv")
            h = self.sem["dv"]
            self.cnt["dv"] += 1
            self.ops[eng].append(lambda e, fn=fn, h=h: fn(e).then_inc(h, 1))
            return self.cnt["dv"]
        if inc is None:
            self.ops[eng].append(fn)
            return None
        h = self.sem[inc]
        self.cnt[inc] += amt
        self.ops[eng].append(lambda e, fn=fn, h=h, a=amt: fn(e).then_inc(h, a))
        return self.cnt[inc]

    def barrier(self):
        for name, issuer in self.dma_sems.items():
            if self.cnt[name]:
                self.wait(issuer, name)
        self.wait("gpsimd", "cc")
        self.wait("vector", "dv")
        self.wait("scalar", "ac")
        self.wait("tensor", "pe")
        for e in self.ENGS:
            h = self.sem["b_" + e]
            self.cnt["b_" + e] += 1
            self.ops[e].append(lambda eng, h=h: eng.nop().then_inc(h, 1))
        for e in self.ENGS:
            for o in self.ENGS:
                if o != e:
                    self.wait(e, "b_" + o)


def build_program():
    nc = bass.Bass("TRN2", target_bir_lowering=False)
    dt = nc.dram_tensor

    x_own = dt("x_own", [B, H, RPC], F32, kind="ExternalInput").ap()
    biasT_own = dt("biasT_own", [T, RPC], F32, kind="ExternalInput").ap()
    sinT_in = dt("sinT", [128, RPC], F32, kind="ExternalInput").ap()
    cosT_in = dt("cosT", [128, RPC], F32, kind="ExternalInput").ap()
    ident_in = dt("ident_in", [128, 128], F32, kind="ExternalInput").ap()

    wuT = dt("wuT", [H, E], F32, kind="ExternalInput").ap()
    wvT = dt("wvT", [H, E], F32, kind="ExternalInput").ap()
    wbT = dt("wbT", [H, S], F32, kind="ExternalInput").ap()
    wbTs = dt("wbTs", [H, S], F32, kind="ExternalInput").ap()
    owT = dt("owT", [E, H], F32, kind="ExternalInput").ap()
    woT = dt("woT", [H, H], F32, kind="ExternalInput").ap()
    gateT = {g: dt(g + "T", [H, H], F32, kind="ExternalInput").ap()
             for g in ("wf", "uf", "wi", "ui", "wc", "uc", "uo")}

    vec_in = {}
    for name, n in (("ln_w", H), ("ln_b", H), ("o_b", H), ("bf", H), ("bi", H),
                    ("bc", H), ("bo", H), ("in_c", H), ("uvb_u", E), ("uvb_b", S), ("uvb_bs", S)):
        vec_in[name] = dt(name, [n], F32, kind="ExternalInput").ap()
    uvb_v_in = dt("uvb_v", [E], F32, kind="ExternalInput").ap()
    gamma_in = dt("gamma", [2, 128], F32, kind="ExternalInput").ap()
    gammas_in = dt("gammas", [2, 128], F32, kind="ExternalInput").ap()
    beta_in = dt("beta", [2, 128], F32, kind="ExternalInput").ap()
    betas_in = dt("betas", [2, 128], F32, kind="ExternalInput").ap()

    y_own = dt("y_own", [B, H, RPC], F32, kind="ExternalOutput").ap()

    wu_h = dt("wu_h", [H, E], BF16)
    wv_h = dt("wv_h", [H, E], BF16)
    wb_h = dt("wb_h", [H, S], BF16)
    wbs_h = dt("wbs_h", [H, S], BF16)
    ow_h = dt("ow_h", [E, H], BF16)
    uo_h = dt("uo_h", [H, H], BF16)
    ag_in = [dt(f"ag_in{b}", [RPCB, E], BF16) for b in range(B)]
    ag_out = [dt(f"ag_out{b}", [NCORES * RPCB, E], BF16) for b in range(B)]

    from contextlib import ExitStack
    es = ExitStack()
    with es:
        e = es.enter_context

        xnT = e(nc.sbuf_tensor("xnT", [128, KC * B * RPC], BF16))
        bb = [e(nc.sbuf_tensor(f"bb{i}", [128, 8192], BF16)) for i in range(6)]
        aux_h = e(nc.sbuf_tensor("aux_h", [128, 3072], BF16))
        c_f32 = e(nc.sbuf_tensor("c_f32", [128, KC * RPC], F32))
        c_bfA = e(nc.sbuf_tensor("c_bfA", [128, KC * RPC], BF16))
        c_bfB = e(nc.sbuf_tensor("c_bfB", [128, KC * RPC], BF16))
        scr = e(nc.sbuf_tensor("scr", [128, 512], F32))
        scr2 = e(nc.sbuf_tensor("scr2", [128, 512], F32))
        gates = [e(nc.sbuf_tensor(f"gate{i}", [128, RPC], F32)) for i in range(3)]
        upd = e(nc.sbuf_tensor("upd", [128, RPC], F32))
        sq_h = e(nc.sbuf_tensor("sq_h", [128, RPC], BF16))
        tmp_h = e(nc.sbuf_tensor("tmp_h", [128, RPC], BF16))
        muB = e(nc.sbuf_tensor("muB", [128, RPC], BF16))
        rsB = e(nc.sbuf_tensor("rsB", [128, RPC], BF16))
        hT_b = e(nc.sbuf_tensor("hT_b", [128, RPC], BF16))
        xres = [e(nc.sbuf_tensor(f"xres{i}", [128, RPC], F32)) for i in range(2)]
        outs = [e(nc.sbuf_tensor(f"outs{i}", [128, RPC], F32)) for i in range(2)]
        stat = e(nc.sbuf_tensor("stat", [1, 2 * RPC], F32))
        uvbv_row = e(nc.sbuf_tensor("uvbv_row", [1, E], BF16))
        ones_col = e(nc.sbuf_tensor("ones_col", [128, 1], BF16))
        ones_row = e(nc.sbuf_tensor("ones_row", [1, 128], BF16))
        ones_row_f = e(nc.sbuf_tensor("ones_row_f", [1, 128], F32))
        ident_h = e(nc.sbuf_tensor("ident_h", [128, 128], BF16))
        sin_sb = e(nc.sbuf_tensor("sin_sb", [128, RPC], F32))
        cos_sb = e(nc.sbuf_tensor("cos_sb", [128, RPC], F32))
        sm = e(nc.sbuf_tensor("sm", [128, 96], F32))

        p_m = [e(nc.psum_tensor(f"p_m{i}", [128, RPC], F32)) for i in range(3)]
        p_v = [e(nc.psum_tensor(f"p_v{i}", [128, 512], F32)) for i in range(2)]
        p_s = [e(nc.psum_tensor(f"p_s{i}", [1, RPC], F32)) for i in range(2)]
        p_b = e(nc.psum_tensor("p_b", [128, 512], F32))

        P = Prog(nc, es)
        psum_owner = {}

        def group_begin(key):
            if key in psum_owner:
                s, v = psum_owner[key]
                P.wait("tensor", s, v)

        smcol = {}
        _c = [0]
        def alloc_sm(name, ncols):
            smcol[name] = _c[0]
            _c[0] += ncols
        for name in ("ln_w", "ln_b", "o_b", "bf", "bi", "bc", "bo", "in_c"):
            alloc_sm(name, KC)
        alloc_sm("uvb_u", ET)
        alloc_sm("uvb_b", 1)
        alloc_sm("uvb_bs", 1)
        alloc_sm("gamma", 2)
        alloc_sm("beta", 2)
        alloc_sm("gammas", 2)
        alloc_sm("betas", 2)
        alloc_sm("g0p", 1)
        alloc_sm("b0p", 1)
        alloc_sm("g0ps", 1)
        alloc_sm("b0ps", 1)
        assert _c[0] <= 96

        def smc(name, i=0):
            return sm[:, smcol[name] + i: smcol[name] + i + 1]

        WG = {g: bb[i] for i, g in enumerate(("wf", "uf", "wi", "ui", "wc", "uc"))}
        kernelT = bb[0]
        kT_sb = bb[0][:, 4096:6144]
        qT_sb = bb[0][:, 6144:6400]
        uT = bb[1][:, 0:4096]
        eT = bb[1][:, 4096:8192]
        biasT_sb = bb[2][:, 0:4096]
        uocT = bb[2][:, 4096:6144]
        xnnT = bb[2][:, 6144:8192]
        ow_strip = [bb[3][:, i * 2048:(i + 1) * 2048] for i in range(2)]
        woT_sb = bb[4]
        wu_strip = [bb[5][:, i * 1024:(i + 1) * 1024] for i in range(2)]
        wv_buf = [bb[5][:, 2048 + i * 512: 2048 + (i + 1) * 512] for i in range(2)]
        wb_buf = bb[5][:, 3072:4096]
        vstrip = [bb[5][:, 4096 + i * 2048: 4096 + (i + 1) * 2048] for i in range(2)]
        vout = [aux_h[:, i * 512:(i + 1) * 512] for i in range(2)]
        uo_strip = [aux_h[:, 1024 + i * 1024: 1024 + (i + 1) * 1024] for i in range(2)]
        x_bf = c_bfA
        uvbv_hlp = c_bfA
        cast_h = c_bfB

        import dataclasses as _dc

        def mkap(base_ap, offset, dims):
            return _dc.replace(base_ap, offset=offset,
                               ap=type(base_ap.ap)(dims))

        def dma(eng, sem, out, in_, slow=False):
            P.dsem(sem, eng)
            if slow:
                return P.emit(eng, lambda e, o=out, i=in_: e.dma_start(
                    out=o, in_=i, allow_slow_non_contiguous=True), inc=sem, amt=16)
            return P.emit(eng, lambda e, o=out, i=in_: e.dma_start(out=o, in_=i),
                          inc=sem, amt=16)

        for name in ("ln_w", "ln_b", "o_b", "bf", "bi", "bc", "bo", "in_c"):
            dma("sync", "LS", sm[:, smcol[name]:smcol[name] + KC],
                vec_in[name].rearrange("(t p) -> p t", p=128), slow=True)
        dma("sync", "LS", sm[:, smcol["uvb_u"]:smcol["uvb_u"] + ET],
            vec_in["uvb_u"].rearrange("(t p) -> p t", p=128), slow=True)
        dma("sync", "LS", sm[:, smcol["uvb_b"]:smcol["uvb_b"] + 1],
            vec_in["uvb_b"].rearrange("(t p) -> p t", p=128), slow=True)
        dma("sync", "LS", sm[:, smcol["uvb_bs"]:smcol["uvb_bs"] + 1],
            vec_in["uvb_bs"].rearrange("(t p) -> p t", p=128), slow=True)
        dma("sync", "LS", sm[:, smcol["gamma"]:smcol["gamma"] + 2],
            gamma_in.rearrange("h s -> s h"), slow=True)
        dma("sync", "LS", sm[:, smcol["beta"]:smcol["beta"] + 2],
            beta_in.rearrange("h s -> s h"), slow=True)
        dma("sync", "LS", sm[:, smcol["gammas"]:smcol["gammas"] + 2],
            gammas_in.rearrange("h s -> s h"), slow=True)
        dma("sync", "LS", sm[:, smcol["betas"]:smcol["betas"] + 2],
            betas_in.rearrange("h s -> s h"), slow=True)
        dma("sync", "LS", scr[:, :128], ident_in)
        dma("sync", "LS", sin_sb[:, :], sinT_in)
        dma("sync", "LS", cos_sb[:, :], cosT_in)

        P.wait("vector", "LS")
        P.emit("vector", lambda e: e.memset(ones_col[:, :], 1.0))
        P.emit("vector", lambda e: e.memset(ones_row[:, :], 1.0))
        P.emit("vector", lambda e: e.memset(ones_row_f[:, :], 1.0))
        P.emit("vector", lambda e: e.tensor_copy(ident_h[:, :], scr[:, :128]))
        for q in range(4):
            dma("sync", "L1", scr2[0:1, :512],
                uvb_v_in[q * 512:(q + 1) * 512].rearrange("(a m) -> a m", a=1))
            P.wait("vector", "L1")
            dvq = P.emit("vector", lambda e, q_=q: e.tensor_copy(
                uvbv_hlp[0:1, q_ * 512:(q_ + 1) * 512], scr2[0:1, :512]), inc="dv")
            P.wait("sync", "dv", dvq)
        P.wait("gpsimd", "dv")
        dma("gpsimd", "SU", uvbv_row[0:1, :], uvbv_hlp[0:1, 0:E])
        P.emit("vector", lambda e: e.tensor_scalar_mul(smc("g0p"), smc("gamma", 0), INV_SQRT_S))
        P.emit("vector", lambda e: e.tensor_scalar_mul(smc("b0p"), smc("beta", 0), INV_SQRT_S))
        P.emit("vector", lambda e: e.tensor_scalar_mul(smc("g0ps"), smc("gammas", 0), INV_SQRT_S))
        P.emit("vector", lambda e: e.tensor_scalar_mul(smc("b0ps"), smc("betas", 0), INV_SQRT_S),
               inc="dv")

        scr_guard = [0, 0]

        def cast_chunks(chunks):
            pass

        def cast_resident(src_ap, dst, n_out):
            i = 0
            for kc in range(KC):
                for off in range(0, n_out, 512):
                    wdt = min(512, n_out - off)
                    par = i % 2
                    buf = (scr, scr2)[par]
                    if scr_guard[par]:
                        P.wait("sync", "dv", scr_guard[par])
                    dma("sync", "L%d" % par, buf[:, :wdt],
                        src_ap[kc * 128:(kc + 1) * 128, off:off + wdt])
                    P.wait("vector", "L%d" % par)
                    dvv = P.emit("vector", lambda e, b=buf[:, :wdt],
                                 d=dst[:, kc * n_out + off: kc * n_out + off + wdt]:
                                 e.tensor_copy(d, b), inc="dv")
                    scr_guard[par] = dvv
                    i += 1

        cast_dram_guard = [0, 0]

        def cast_to_dram(src_ap, dst_dram, rows, cols):
            i = 0
            for r0 in range(0, rows, 128):
                for off in range(0, cols, 512):
                    wdt = min(512, cols - off)
                    par = i % 2
                    buf = (scr, scr2)[par]
                    if scr_guard[par]:
                        P.wait("sync", "dv", scr_guard[par])
                    dma("sync", "L%d" % par, buf[:, :wdt],
                        src_ap[r0:r0 + 128, off:off + wdt])
                    hbuf = cast_h[:, par * 1024: par * 1024 + wdt]
                    P.wait("vector", "L%d" % par)
                    if cast_dram_guard[par]:
                        P.wait("vector", "SC%d" % par)
                    dvv = P.emit("vector", lambda e, b=buf[:, :wdt], h=hbuf:
                                 e.tensor_copy(h, b), inc="dv")
                    scr_guard[par] = dvv
                    P.wait("gpsimd", "dv", dvv)
                    dma("gpsimd", "SC%d" % par, dst_dram.ap()[r0:r0 + 128, off:off + wdt], hbuf)
                    cast_dram_guard[par] = 1
                    i += 1

        for g in ("wf", "uf", "wi", "ui", "wc", "uc"):
            cast_resident(gateT[g], WG[g], H)
        cast_to_dram(gateT["uo"], uo_h, H, H)
        cast_to_dram(wuT, wu_h, H, E)
        cast_to_dram(wvT, wv_h, H, E)
        cast_to_dram(wbT, wb_h, H, S)
        cast_to_dram(wbTs, wbs_h, H, S)
        cast_to_dram(owT, ow_h, E, H)

        P.barrier()

        st_mu = stat[0:1, 0:RPC]
        st_b = stat[0:1, RPC:2 * RPC]

        def ln_stats(src_slices):
            P.wait("tensor", "dv")
            P.wait("scalar", "dv")
            for kc in range(KC):
                P.emit("tensor", lambda e, s=src_slices[kc], k=kc:
                       e.matmul(p_s[0][:, :], ones_col[:, :], s,
                                start=(k == 0), stop=(k == KC - 1)),
                       inc="pe")
            for kc in range(KC):
                sq = (sq_h, tmp_h)[kc % 2]
                if kc >= 2:
                    P.wait("scalar", "pe", pe_sq[kc - 2])
                av = P.emit("scalar", lambda e, s=src_slices[kc], q=sq:
                            e.activation(out=q[:, :], in_=s, func=AF.Square), inc="ac")
                P.wait("tensor", "ac", av)
                pv = P.emit("tensor", lambda e, k=kc, q=sq:
                            e.matmul(p_s[1][:, :], ones_col[:, :], q[:, :],
                                     start=(k == 0), stop=(k == KC - 1)),
                            inc="pe")
                if kc == 0:
                    pe_sq = {}
                pe_sq[kc] = pv
            P.wait("vector", "pe")
            P.emit("vector", lambda e: e.tensor_scalar_mul(st_mu, p_s[0][0:1, :], 1.0 / H))
            P.emit("vector", lambda e: e.tensor_scalar_mul(st_b, p_s[1][0:1, :], 1.0 / H))
            P.emit("vector", lambda e: e.tensor_mul(p_s[0][0:1, :], st_mu, st_mu))
            P.emit("vector", lambda e: e.tensor_sub(st_b, st_b, p_s[0][0:1, :]))
            P.emit("vector", lambda e: e.tensor_scalar_add(st_b, st_b, EPS), inc="dv")
            P.wait("scalar", "dv")
            P.emit("scalar", lambda e: e.activation(out=st_b, in_=st_b, func=AF.Sqrt),
                   inc="ac")
            P.wait("vector", "ac")
            P.emit("vector", lambda e: e.reciprocal(st_b, st_b), inc="dv")
            P.wait("tensor", "dv")
            P.emit("tensor", lambda e: e.matmul(p_b[:, 0:RPC], ones_row_f[:, :], st_mu,
                                                start=True, stop=True))
            P.emit("tensor", lambda e: e.matmul(p_b[:, RPC:2 * RPC], ones_row_f[:, :], st_b,
                                                start=True, stop=True), inc="pe")
            P.wait("vector", "pe")
            P.emit("vector", lambda e: e.tensor_copy(muB[:, :], p_b[:, 0:RPC]))
            P.emit("vector", lambda e: e.tensor_copy(rsB[:, :], p_b[:, RPC:2 * RPC]),
                   inc="dv")

        def xnT_sl(kc, t):
            off = kc * (B * RPC) + t * RPC
            return xnT[:, off: off + RPC]

        for t in range(B):
            P.wait("vector", "pe")
            for kc in range(KC):
                par = kc % 2
                buf = (scr, scr2)[par]
                if scr_guard[par]:
                    P.wait("sync", "dv", scr_guard[par])
                dma("sync", "L%d" % par, buf[:, :RPC],
                    x_own[t, kc * 128:(kc + 1) * 128, :])
                P.wait("vector", "L%d" % par)
                dvv = P.emit("vector", lambda e, b=buf[:, :RPC],
                             d=x_bf[:, kc * RPC:(kc + 1) * RPC]:
                             e.tensor_copy(d, b), inc="dv")
                scr_guard[par] = dvv
            xs = [x_bf[:, kc * RPC:(kc + 1) * RPC] for kc in range(KC)]
            ln_stats(xs)
            for kc in range(KC):
                P.emit("vector", lambda e, s=xs[kc]: e.tensor_sub(tmp_h[:, :], s, muB[:, :]))
                P.emit("vector", lambda e, d=xnT_sl(kc, t):
                       e.tensor_mul(d, tmp_h[:, :], rsB[:, :]), inc="dv")

        P.barrier()

        for mt in range(KC):
            P.emit("vector", lambda e, m=mt: e.tensor_scalar(
                out=c_f32[:, m * RPC:(m + 1) * RPC], in0=muB[:, :],
                scalar1=0.0, scalar2=smc("in_c", m), op0=ALU.mult, op1=ALU.add))
        for mt in range(KC):
            P.emit("vector", lambda e, m=mt: e.tensor_copy(
                c_bfA[:, m * RPC:(m + 1) * RPC], c_f32[:, m * RPC:(m + 1) * RPC]),
                inc="dv")

        gatespec = [("wf", "uf", "bf", AF.Sigmoid, 0),
                    ("wi", "ui", "bi", AF.Sigmoid, 1),
                    ("wc", "uc", "bc", AF.Tanh, 2)]
        dv_after_update = P.cnt["dv"]
        for t in range(B):
            cur = c_bfA if t % 2 == 0 else c_bfB
            nxt = c_bfB if t % 2 == 0 else c_bfA
            first_group = True
            for mt in range(KC):
                for (wg, ug, bg, func, gi) in gatespec:
                    key = "m%d" % ((mt * 3 + gi) % 3)
                    pm = p_m[(mt * 3 + gi) % 3]
                    group_begin(key)
                    for kc in range(KC):
                        P.emit("tensor", lambda e, p=pm, w=WG[wg], m=mt, k=kc, t_=t:
                               e.matmul(p[:, :],
                                        w[:, k * H + m * 128: k * H + m * 128 + 128],
                                        xnT_sl(k, t_), start=(k == 0), stop=False))
                    if first_group:
                        P.wait("tensor", "dv", dv_after_update)
                        first_group = False
                    for kc in range(KC):
                        P.emit("tensor", lambda e, p=pm, u=WG[ug], m=mt, k=kc, c=cur:
                               e.matmul(p[:, :],
                                        u[:, k * H + m * 128: k * H + m * 128 + 128],
                                        c[:, k * RPC:(k + 1) * RPC],
                                        start=False, stop=(k == KC - 1)),
                               inc="pe" if kc == KC - 1 else None)
                    pv = P.cnt["pe"]
                    P.wait("scalar", "pe", pv)
                    P.wait("scalar", "dv", dv_after_update if mt == 0 else dv_mt)
                    av = P.emit("scalar", lambda e, p=pm, f=func, b=bg, m=mt, g=gi:
                                e.activation(out=gates[g][:, :], in_=p[:, :], func=f,
                                             bias=smc(b, m)), inc="ac")
                    psum_owner[key] = ("ac", av)
                P.wait("vector", "ac")
                P.emit("vector", lambda e, m=mt: e.tensor_mul(
                    upd[:, :], gates[0][:, :], c_f32[:, m * RPC:(m + 1) * RPC]))
                P.emit("vector", lambda e: e.tensor_mul(
                    gates[2][:, :], gates[1][:, :], gates[2][:, :]))
                P.emit("vector", lambda e, m=mt: e.tensor_add(
                    c_f32[:, m * RPC:(m + 1) * RPC], upd[:, :], gates[2][:, :]))
                dv_mt = P.emit("vector", lambda e, m=mt, n=nxt: e.tensor_copy(
                    n[:, m * RPC:(m + 1) * RPC], c_f32[:, m * RPC:(m + 1) * RPC]),
                    inc="dv")
            dv_after_update = P.cnt["dv"]

        c_fin = c_bfA if B % 2 == 0 else c_bfB

        P.barrier()

        uoc_pe = {}
        for ot in range(KC):
            ustr = uo_strip[ot % 2]
            if ot >= 2:
                P.wait("sync", "pe", uoc_pe[ot - 2])
            dma("sync", "LR%d" % (ot % 2),
                ustr[:, :1024].rearrange("p (mc c) -> p mc c", mc=KC),
                mkap(uo_h.ap(), ot * 128, [[H, 128], [128 * H, KC], [1, 128]]))
            key = "m%d" % (ot % 2)
            pm = p_m[ot % 2]
            group_begin(key)
            P.wait("tensor", "LR%d" % (ot % 2))
            for mc in range(KC):
                P.emit("tensor", lambda e, p=pm, s=ustr, m=mc:
                       e.matmul(p[:, :], s[:, m * 128:(m + 1) * 128],
                                c_fin[:, m * RPC:(m + 1) * RPC],
                                start=(m == 0), stop=(m == KC - 1)),
                       inc="pe" if mc == KC - 1 else None)
            pv = P.cnt["pe"]
            uoc_pe[ot] = pv
            P.wait("vector", "pe", pv)
            dvv = P.emit("vector", lambda e, o=ot, p=pm: e.tensor_copy(
                uocT[:, o * RPC:(o + 1) * RPC], p[:, :]), inc="dv")
            psum_owner[key] = ("dv", dvv)

        P.barrier()

        for jt in range(ET):
            par = jt % 2
            buf = (scr, scr2)[par]
            if scr_guard[par]:
                P.wait("sync", "dv", scr_guard[par])
            dma("sync", "L%d" % par, buf[:, :RPC],
                biasT_own[jt * 128:(jt + 1) * 128, :])
            P.wait("vector", "L%d" % par)
            dvv = P.emit("vector", lambda e, b=buf[:, :RPC], j=jt:
                         e.tensor_copy(biasT_sb[:, j * RPC:(j + 1) * RPC], b), inc="dv")
            scr_guard[par] = dvv
        for kc in range(KC):
            for hh in range(2):
                par = (kc * 2 + hh) % 2
                buf = (scr, scr2)[par]
                if scr_guard[par]:
                    P.wait("sync", "dv", scr_guard[par])
                dma("sync", "L%d" % par, buf[:, :],
                    woT[kc * 128:(kc + 1) * 128, hh * 512:(hh + 1) * 512])
                P.wait("vector", "L%d" % par)
                dvv = P.emit("vector", lambda e, b=buf[:, :], k=kc, h_=hh:
                             e.tensor_copy(woT_sb[:, k * H + h_ * 512:
                                                  k * H + h_ * 512 + 512], b),
                             inc="dv")
                scr_guard[par] = dvv

        P.barrier()

        def xnn_make(b):
            xs = [xnT_sl(kc, b) for kc in range(KC)]
            ln_stats(xs)
            for kc in range(KC):
                P.emit("vector", lambda e, s=xs[kc]: e.tensor_sub(tmp_h[:, :], s, muB[:, :]))
                P.emit("vector", lambda e: e.tensor_mul(sq_h[:, :], tmp_h[:, :], rsB[:, :]))
                P.emit("vector", lambda e, k=kc: e.tensor_scalar(
                    out=xnnT[:, k * RPC:(k + 1) * RPC], in0=sq_h[:, :],
                    scalar1=smc("ln_w", k), scalar2=smc("ln_b", k),
                    op0=ALU.mult, op1=ALU.add), inc="dv")

        base_sw = c_f32[:, 0:RPC]

        def rope_make(dst, gcol, bcol, gscol, bscol, a0, a1):
            P.emit("vector", lambda e, g=gcol, b_=bcol: e.tensor_scalar(
                out=upd[:, :], in0=gates[2][:, :], scalar1=g, scalar2=b_,
                op0=ALU.mult, op1=ALU.add))
            P.emit("vector", lambda e, g=gscol, b_=bscol: e.tensor_scalar(
                out=a0, in0=base_sw, scalar1=g, scalar2=b_,
                op0=ALU.mult, op1=ALU.add))
            P.emit("vector", lambda e: e.tensor_mul(a1, a0, sin_sb[:, :]))
            P.emit("vector", lambda e: e.tensor_mul(upd[:, :], upd[:, :], cos_sb[:, :]))
            return P.emit("vector", lambda e, d=dst:
                          e.tensor_add(d, upd[:, :], a1), inc="dv")

        vout_guard = [0, 0]
        out_guard = [0, 0]

        def produce(b):
            xnn_make(b)
            dv_xnn = P.cnt["dv"]
            for et in range(ET):
                sb = wu_strip[et % 2]
                if et >= 2:
                    P.wait("sync", "pe", u_pe[et - 2])
                dma("sync", "LU%d" % (et % 2),
                    sb[:, :1024].rearrange("p (kc c) -> p kc c", kc=KC),
                    mkap(wu_h.ap(), et * 128, [[E, 128], [128 * E, KC], [1, 128]]))
                key = "m%d" % (et % 2)
                pm = p_m[et % 2]
                group_begin(key)
                P.wait("tensor", "LU%d" % (et % 2))
                if et == 0:
                    P.wait("tensor", "dv", dv_xnn)
                for kc in range(KC):
                    pv = P.emit("tensor", lambda e, p=pm, s=sb, k=kc:
                                e.matmul(p[:, :], s[:, k * 128:(k + 1) * 128],
                                         xnnT[:, k * RPC:(k + 1) * RPC],
                                         start=(k == 0), stop=(k == KC - 1)),
                                inc="pe")
                if et == 0:
                    u_pe = {}
                u_pe[et] = pv
                P.wait("scalar", "pe", pv)
                av = P.emit("scalar", lambda e, p=pm, t_=et: e.activation(
                    out=uT[:, t_ * RPC:(t_ + 1) * RPC], in_=p[:, :], func=AF.Silu,
                    bias=smc("uvb_u", t_)), inc="ac")
                psum_owner[key] = ("ac", av)
            for swp, w_src, dst_sb, bias_name in (
                    (0, wb_h, gates[2][:, :], "uvb_b"),
                    (1, None, base_sw, "uvb_bs")):
                src = (wb_h if swp == 0 else wbs_h).ap()
                dma("sync", "LB",
                    wb_buf[:, :1024].rearrange("p (kc c) -> p kc c", kc=KC),
                    mkap(src, 0, [[S, 128], [128 * S, KC], [1, S]]))
                group_begin("m2")
                P.wait("tensor", "LB")
                for kc in range(KC):
                    P.emit("tensor", lambda e, k=kc:
                           e.matmul(p_m[2][:, :], wb_buf[:, k * 128:(k + 1) * 128],
                                    xnnT[:, k * RPC:(k + 1) * RPC],
                                    start=(k == 0), stop=(k == KC - 1)),
                           inc="pe" if kc == KC - 1 else None)
                pv = P.cnt["pe"]
                P.wait("sync", "pe", pv)
                P.wait("scalar", "pe", pv)
                av = P.emit("scalar", lambda e, d=dst_sb, bn=bias_name: e.activation(
                    out=d, in_=p_m[2][:, :], func=AF.Silu,
                    bias=smc(bn)), inc="ac")
                psum_owner["m2"] = ("ac", av)
            P.wait("vector", "ac", av)
            a0 = gates[0][:, :]
            a1 = gates[1][:, :]
            rope_make(qT_sb[:, :], smc("g0p"), smc("b0p"),
                      smc("g0ps"), smc("b0ps"), a0, a1)
            kv = rope_make(tmp_h[:, :], smc("gamma", 1), smc("beta", 1),
                           smc("gammas", 1), smc("betas", 1), a0, a1)
            P.wait("gpsimd", "dv", kv)
            dma("gpsimd", "SK",
                ag_in[b].ap()[RPC:RPC + 16, :].rearrange("a (c i) -> (a c) i", i=RPC),
                tmp_h[:, :])
            for nch in range(4):
                for kc in range(KC):
                    par = (nch * KC + kc) % 2
                    wb2 = wv_buf[par]
                    if nch * KC + kc >= 2:
                        P.wait("sync", "pe", v_pe[nch * KC + kc - 2])
                    dma("sync", "LV%d" % par, wb2[:, :],
                        wv_h.ap()[kc * 128:(kc + 1) * 128, nch * 512:(nch + 1) * 512])
                    for rt in range(2):
                        key = "v%d" % rt
                        if kc == 0:
                            group_begin(key)
                            P.emit("tensor", lambda e, p=p_v[rt], n=nch: e.matmul(
                                p[:, :], ones_row[:, :],
                                uvbv_row[0:1, n * 512:(n + 1) * 512],
                                start=True, stop=False))
                        P.wait("tensor", "LV%d" % par)
                        pv = P.emit("tensor", lambda e, p=p_v[rt], w=wb2, r=rt, k=kc:
                                    e.matmul(p[:, :],
                                             xnnT[:, k * RPC + r * 128: k * RPC + r * 128 + 128],
                                             w[:, :], start=False, stop=(k == KC - 1)),
                                    inc="pe")
                        if nch == 0 and kc == 0 and rt == 0:
                            v_pe = {}
                        v_pe[nch * KC + kc] = pv
                        if kc == KC - 1:
                            vo = vout[rt]
                            P.wait("scalar", "pe", pv)
                            if vout_guard[rt]:
                                P.wait("scalar", "SV%d" % rt)
                            av = P.emit("scalar", lambda e, p=p_v[rt], o=vo: e.activation(
                                out=o, in_=p[:, :], func=AF.Silu), inc="ac")
                            psum_owner[key] = ("ac", av)
                            P.wait("gpsimd", "ac", av)
                            dma("gpsimd", "SV%d" % rt,
                                ag_in[b].ap()[rt * 128:(rt + 1) * 128,
                                              nch * 512:(nch + 1) * 512], vo)
                            vout_guard[rt] = 1
            P.wait("gpsimd", "SV0")
            P.wait("gpsimd", "SV1")
            P.wait("gpsimd", "SK")
            P.emit("gpsimd", lambda e, b_=b: e.collective_compute(
                "AllGather", ALU.bypass, replica_groups=[list(range(NCORES))],
                ins=[ag_in[b_].ap().opt()], outs=[ag_out[b_].ap().opt()]), inc="cc")

        def consume(b):
            P.wait("gpsimd", "cc")
            for r in range(NCORES):
                blk = ag_out[b].ap()[r * RPCB + RPC: r * RPCB + RPC + 16, :]
                dma("gpsimd", "LK", kT_sb[:, r * RPC:(r + 1) * RPC],
                    blk.rearrange("a (c i) -> (a c) i", i=RPC))
            for jt in range(ET):
                key = "m%d" % (jt % 2)
                pm = p_m[jt % 2]
                group_begin(key)
                if jt == 0:
                    P.wait("tensor", "LK")
                P.emit("tensor", lambda e, p=pm, j=jt: e.matmul(
                    p[:, :], ident_h[:, :], biasT_sb[:, j * RPC:(j + 1) * RPC],
                    start=True, stop=False))
                pv = P.emit("tensor", lambda e, p=pm, j=jt: e.matmul(
                    p[:, :], kT_sb[:, j * 128:(j + 1) * 128], qT_sb,
                    start=False, stop=True), inc="pe")
                P.wait("scalar", "pe", pv)
                if jt > 0:
                    P.wait("scalar", "dv", sq_dv)
                av = P.emit("scalar", lambda e, p=pm: e.activation(
                    out=upd[:, :], in_=p[:, :], func=AF.Relu), inc="ac")
                psum_owner[key] = ("ac", av)
                P.wait("vector", "ac", av)
                sq_dv = P.emit("vector", lambda e, j=jt: e.tensor_mul(
                    kernelT[:, j * RPC:(j + 1) * RPC], upd[:, :], upd[:, :]), inc="dv")
            dv_kernel = P.cnt["dv"]
            for et in range(ET):
                vs = vstrip[et % 2]
                if et >= 2:
                    P.wait("gpsimd", "pe", kv_pe[et - 2])
                for q in range(2):
                    dma("gpsimd", "LW%d" % (et % 2),
                        mkap(vs, vs.offset + q * 128,
                             [[8192, 128], [256, NCORES], [1, 128]]),
                        mkap(ag_out[b].ap(), et * 128 + q * 128 * E,
                             [[E, 128], [RPCB * E, NCORES], [1, 128]]))
                key = "m%d" % (et % 2)
                pm = p_m[et % 2]
                group_begin(key)
                P.wait("tensor", "LW%d" % (et % 2))
                if et == 0:
                    P.wait("tensor", "dv", dv_kernel)
                for jt in range(ET):
                    pv = P.emit("tensor", lambda e, p=pm, v_=vs, j=jt: e.matmul(
                        p[:, :], v_[:, j * 128:(j + 1) * 128],
                        kernelT[:, j * RPC:(j + 1) * RPC],
                        start=(j == 0), stop=(j == ET - 1)),
                        inc="pe" if jt == ET - 1 else None)
                if et == 0:
                    kv_pe = {}
                kv_pe[et] = pv
                P.wait("vector", "pe", pv)
                dvv = P.emit("vector", lambda e, p=pm, t_=et: e.tensor_mul(
                    eT[:, t_ * RPC:(t_ + 1) * RPC], p[:, :],
                    uT[:, t_ * RPC:(t_ + 1) * RPC]), inc="dv")
                psum_owner[key] = ("dv", dvv)
            dv_eT = P.cnt["dv"]
            for ft in range(KC):
                osb = ow_strip[ft % 2]
                if ft >= 2:
                    P.wait("sync", "pe", o_pe[ft - 2])
                dma("sync", "LO%d" % (ft % 2),
                    osb[:, :2048].rearrange("p (ec c) -> p ec c", ec=ET),
                    mkap(ow_h.ap(), ft * 128, [[H, 128], [128 * H, ET], [1, 128]]))
                group_begin("m2")
                P.emit("tensor", lambda e, f=ft: e.matmul(
                    p_m[2][:, :], ident_h[:, :], uocT[:, f * RPC:(f + 1) * RPC],
                    start=True, stop=False))
                for kc in range(KC):
                    P.emit("tensor", lambda e, f=ft, k=kc, b_=b: e.matmul(
                        p_m[2][:, :], woT_sb[:, k * H + f * 128: k * H + f * 128 + 128],
                        xnT_sl(k, b_), start=False, stop=(k == KC - 1)),
                        inc="pe" if kc == KC - 1 else None)
                pv_h = P.cnt["pe"]
                P.wait("scalar", "pe", pv_h)
                if ft > 0:
                    P.wait("scalar", "dv", comb_dv)
                av = P.emit("scalar", lambda e, f=ft: e.activation(
                    out=hT_b[:, :], in_=p_m[2][:, :], func=AF.Tanh, bias=smc("bo", f)),
                    inc="ac")
                psum_owner["m2"] = ("ac", av)
                key = "m%d" % (ft % 2)
                pm = p_m[ft % 2]
                group_begin(key)
                P.emit("tensor", lambda e, p=pm, f=ft, b_=b: e.matmul(
                    p[:, :], ident_h[:, :], xnT_sl(f, b_), start=True, stop=False))
                P.wait("tensor", "LO%d" % (ft % 2))
                if ft == 0:
                    P.wait("tensor", "dv", dv_eT)
                for ec in range(ET):
                    P.emit("tensor", lambda e, p=pm, s=osb, c_=ec: e.matmul(
                        p[:, :], s[:, c_ * 128:(c_ + 1) * 128],
                        eT[:, c_ * RPC:(c_ + 1) * RPC],
                        start=False, stop=(c_ == ET - 1)),
                        inc="pe" if ec == ET - 1 else None)
                pv_o = P.cnt["pe"]
                if ft == 0:
                    o_pe = {}
                o_pe[ft] = pv_o
                xr = xres[ft % 2]
                dma("sync", "LX%d" % (ft % 2), xr[:, :],
                    x_own[b, ft * 128:(ft + 1) * 128, :])
                ob = outs[ft % 2]
                P.wait("vector", "pe", pv_o)
                P.wait("vector", "ac", av)
                if out_guard[ft % 2]:
                    P.wait("vector", "SO%d" % (ft % 2))
                P.emit("vector", lambda e, p=pm, f=ft, o=ob: e.scalar_tensor_tensor(
                    out=o[:, :], in0=p[:, :], scalar=smc("o_b", f), in1=hT_b[:, :],
                    op0=ALU.add, op1=ALU.add), inc="dv")
                psum_owner[key] = ("dv", P.cnt["dv"])
                P.wait("vector", "LX%d" % (ft % 2))
                comb_dv = P.emit("vector", lambda e, o=ob, x=xr: e.tensor_add(
                    o[:, :], o[:, :], x[:, :]), inc="dv")
                P.wait("gpsimd", "dv", comb_dv)
                dma("gpsimd", "SO%d" % (ft % 2), y_own[b, ft * 128:(ft + 1) * 128, :],
                    ob[:, :])
                out_guard[ft % 2] = 1
                P.wait("sync", "dv", comb_dv)

        for b in range(B):
            produce(b)
            consume(b)
            P.barrier()

        P.wait("gpsimd", "SO0")
        P.wait("gpsimd", "SO1")

        block = e(nc.Block())

        @block.sync
        def _(sync):
            for fn in P.ops["sync"]:
                fn(sync)

        @block.gpsimd
        def _(gpsimd):
            for fn in P.ops["gpsimd"]:
                fn(gpsimd)

        @block.tensor
        def _(tensor):
            for fn in P.ops["tensor"]:
                fn(tensor)

        @block.vector
        def _(vector):
            for fn in P.ops["vector"]:
                fn(vector)

        @block.scalar
        def _(scalar):
            for fn in P.ops["scalar"]:
                fn(scalar)

    return nc


_NC_CACHE = None


def kernel(**inputs):
    global _NC_CACHE
    x = np.asarray(inputs["L0_attn_x"], dtype=np.float32)
    sin_t, cos_t = _rope_tables()
    biasT = _bias_matrix(np.asarray(inputs["L0_attn_w"], np.float32))
    uv_w = np.asarray(inputs["L0_attn_uv_w"], np.float32)
    uv_b = np.asarray(inputs["L0_attn_uv_b"], np.float32)

    common = dict(
        ident_in=np.eye(128, dtype=np.float32),
        wuT=np.ascontiguousarray(uv_w[:E].T),
        wvT=np.ascontiguousarray(uv_w[E:2 * E].T),
        wbT=np.ascontiguousarray(uv_w[2 * E:].T),
        wbTs=np.ascontiguousarray(uv_w[2 * E:].T[:, _PERM]),
        owT=np.ascontiguousarray(np.asarray(inputs["L0_attn_o_w"], np.float32).T),
        woT=np.ascontiguousarray(np.asarray(inputs["L0_ffn_wo"], np.float32).T),
        wfT=np.ascontiguousarray(np.asarray(inputs["L0_ffn_wf"], np.float32).T),
        ufT=np.ascontiguousarray(np.asarray(inputs["L0_ffn_uf"], np.float32).T),
        wiT=np.ascontiguousarray(np.asarray(inputs["L0_ffn_wi"], np.float32).T),
        uiT=np.ascontiguousarray(np.asarray(inputs["L0_ffn_ui"], np.float32).T),
        wcT=np.ascontiguousarray(np.asarray(inputs["L0_ffn_wc"], np.float32).T),
        ucT=np.ascontiguousarray(np.asarray(inputs["L0_ffn_uc"], np.float32).T),
        uoT=np.ascontiguousarray(np.asarray(inputs["L0_ffn_uo"], np.float32).T),
        ln_w=np.asarray(inputs["L0_attn_ln_w"], np.float32),
        ln_b=np.asarray(inputs["L0_attn_ln_b"], np.float32),
        o_b=np.asarray(inputs["L0_attn_o_b"], np.float32),
        bf=np.asarray(inputs["L0_ffn_bf"], np.float32),
        bi=np.asarray(inputs["L0_ffn_bi"], np.float32),
        bc=np.asarray(inputs["L0_ffn_bc"], np.float32),
        bo=np.asarray(inputs["L0_ffn_bo"], np.float32),
        in_c=np.asarray(inputs["L0_ffn_in_c"], np.float32),
        uvb_u=np.ascontiguousarray(uv_b[:E]),
        uvb_v=np.ascontiguousarray(uv_b[E:2 * E]),
        uvb_b=np.ascontiguousarray(uv_b[2 * E:]),
        uvb_bs=np.ascontiguousarray(uv_b[2 * E:][_PERM]),
        gamma=np.asarray(inputs["L0_attn_gamma"], np.float32),
        beta=np.asarray(inputs["L0_attn_beta"], np.float32),
        gammas=np.ascontiguousarray(np.asarray(inputs["L0_attn_gamma"], np.float32)[:, _PERM]),
        betas=np.ascontiguousarray(np.asarray(inputs["L0_attn_beta"], np.float32)[:, _PERM]),
    )

    xT = np.ascontiguousarray(x.transpose(0, 2, 1))
    in_maps = []
    for c in range(NCORES):
        r0, r1 = c * RPC, (c + 1) * RPC
        m = dict(common)
        m["x_own"] = np.ascontiguousarray(xT[:, :, r0:r1])
        m["biasT_own"] = np.ascontiguousarray(biasT[:, r0:r1])
        st = sin_t[r0:r1].T
        ct = cos_t[r0:r1].T
        m["sinT"] = np.ascontiguousarray(np.concatenate([-st, st], axis=0))
        m["cosT"] = np.ascontiguousarray(np.concatenate([ct, ct], axis=0))
        in_maps.append(m)

    if _NC_CACHE is None:
        _NC_CACHE = build_program()
    res = run_bass_kernel_spmd(_NC_CACHE, in_maps, core_ids=list(range(NCORES)))
    shards = [res.results[c]["y_own"] for c in range(NCORES)]
    outT = np.concatenate(shards, axis=2)
    return np.ascontiguousarray(outT.transpose(0, 2, 1)).astype(np.float32)


# version 07
# speedup vs baseline: 1.4613x; 1.0801x over previous
import numpy as np

import concourse.bass as bass
import concourse.mybir as mybir
from concourse.bass_utils import run_bass_kernel_spmd

F32 = mybir.dt.float32
BF16 = mybir.dt.bfloat16
AF = mybir.ActivationFunctionType
ALU = mybir.AluOpType

NCORES = 8
B = 16
T = 2048
H = 1024
E = 2048
S = 128
RPC = T // NCORES
L = 512
INV_SQRT_S = 1.0 / float(np.sqrt(S))
EPS = 1e-5
KC = H // 128
ET = E // 128
RPCB = RPC + 16
_PERM = np.r_[64:128, 0:64]


def _rope_tables():
    try:
        import jax
        cpu = jax.devices("cpu")[0]
        import jax.numpy as jnp
        with jax.default_device(cpu):
            pos = jnp.arange(T, dtype=jnp.float32)
            inv_freq = 10000.0 ** (jnp.arange(64, dtype=jnp.float32) / 64)
            sinusoid = pos[:, None] * inv_freq[None, :]
            return (np.asarray(jnp.sin(sinusoid)), np.asarray(jnp.cos(sinusoid)))
    except Exception:
        pos = np.arange(T, dtype=np.float32)
        inv_freq = (10000.0 ** (np.arange(64, dtype=np.float32) /
                                np.float32(64))).astype(np.float32)
        sinusoid = pos[:, None] * inv_freq[None, :]
        return np.sin(sinusoid).astype(np.float32), np.cos(sinusoid).astype(np.float32)


def _bias_matrix(w):
    n = w.shape[0]
    rel = np.arange(T)[None, :] - np.arange(T)[:, None] + (L - 1)
    idx = np.where(rel < 0, rel + n, rel)
    idx = np.clip(idx, 0, n - 1)
    return np.ascontiguousarray(w[idx].T)


class Prog:
    ENGS = ("sync", "gpsimd", "tensor", "vector", "scalar")

    def __init__(self, nc, es):
        self.nc = nc
        self.es = es
        self.ops = {e: [] for e in self.ENGS}
        self.sem = {}
        self.cnt = {}
        self.dma_sems = {}
        for name in ("cc", "pe", "dv", "ac",
                     "b_sync", "b_gpsimd", "b_tensor", "b_vector", "b_scalar"):
            self.sem[name] = es.enter_context(nc.semaphore(name))
            self.cnt[name] = 0

    def dsem(self, name, issuer):
        if name not in self.sem:
            self.sem[name] = self.es.enter_context(self.nc.semaphore(name))
            self.cnt[name] = 0
            self.dma_sems[name] = issuer
        return name

    def wait(self, eng, sem, val=None):
        v = self.cnt[sem] if val is None else val
        h = self.sem[sem]
        self.ops[eng].append(lambda e, h=h, v=v: e.wait_ge(h, v))

    def emit(self, eng, fn, inc=None, amt=1):
        if eng == "vector":
            self.wait("vector", "dv")
            h = self.sem["dv"]
            self.cnt["dv"] += 1
            self.ops[eng].append(lambda e, fn=fn, h=h: fn(e).then_inc(h, 1))
            return self.cnt["dv"]
        if inc is None:
            self.ops[eng].append(fn)
            return None
        h = self.sem[inc]
        self.cnt[inc] += amt
        self.ops[eng].append(lambda e, fn=fn, h=h, a=amt: fn(e).then_inc(h, a))
        return self.cnt[inc]

    def barrier(self):
        for name, issuer in self.dma_sems.items():
            if self.cnt[name]:
                self.wait(issuer, name)
        self.wait("gpsimd", "cc")
        self.wait("vector", "dv")
        self.wait("scalar", "ac")
        self.wait("tensor", "pe")
        for e in self.ENGS:
            h = self.sem["b_" + e]
            self.cnt["b_" + e] += 1
            self.ops[e].append(lambda eng, h=h: eng.nop().then_inc(h, 1))
        for e in self.ENGS:
            for o in self.ENGS:
                if o != e:
                    self.wait(e, "b_" + o)


def build_program():
    nc = bass.Bass("TRN2", target_bir_lowering=False)
    dt = nc.dram_tensor

    x_own = dt("x_own", [B, H, RPC], F32, kind="ExternalInput").ap()
    biasT_own = dt("biasT_own", [T, RPC], F32, kind="ExternalInput").ap()
    sinT_in = dt("sinT", [128, RPC], F32, kind="ExternalInput").ap()
    cosT_in = dt("cosT", [128, RPC], F32, kind="ExternalInput").ap()
    ident_in = dt("ident_in", [128, 128], F32, kind="ExternalInput").ap()

    wuT = dt("wuT", [H, E], F32, kind="ExternalInput").ap()
    wvT = dt("wvT", [H, E], F32, kind="ExternalInput").ap()
    wbT = dt("wbT", [H, S], F32, kind="ExternalInput").ap()
    wbTs = dt("wbTs", [H, S], F32, kind="ExternalInput").ap()
    owT = dt("owT", [E, H], F32, kind="ExternalInput").ap()
    woT = dt("woT", [H, H], F32, kind="ExternalInput").ap()
    gateT = {g: dt(g + "T", [H, H], F32, kind="ExternalInput").ap()
             for g in ("wf", "uf", "wi", "ui", "wc", "uc", "uo")}

    vec_in = {}
    for name, n in (("ln_w", H), ("ln_b", H), ("o_b", H), ("bf", H), ("bi", H),
                    ("bc", H), ("bo", H), ("in_c", H), ("uvb_u", E), ("uvb_b", S), ("uvb_bs", S)):
        vec_in[name] = dt(name, [n], F32, kind="ExternalInput").ap()
    uvb_v_in = dt("uvb_v", [E], F32, kind="ExternalInput").ap()
    gamma_in = dt("gamma", [2, 128], F32, kind="ExternalInput").ap()
    gammas_in = dt("gammas", [2, 128], F32, kind="ExternalInput").ap()
    beta_in = dt("beta", [2, 128], F32, kind="ExternalInput").ap()
    betas_in = dt("betas", [2, 128], F32, kind="ExternalInput").ap()

    y_own = dt("y_own", [B, H, RPC], F32, kind="ExternalOutput").ap()

    wu_h = dt("wu_h", [H, E], BF16)
    wv_h = dt("wv_h", [H, E], BF16)
    wb_h = dt("wb_h", [H, S], BF16)
    wbs_h = dt("wbs_h", [H, S], BF16)
    ow_h = dt("ow_h", [E, H], BF16)
    uo_h = dt("uo_h", [H, H], BF16)
    ag_in = [dt(f"ag_in{b}", [RPCB, E], BF16) for b in range(B)]
    ag_out = [dt(f"ag_out{b}", [NCORES * RPCB, E], BF16) for b in range(B)]

    from contextlib import ExitStack
    es = ExitStack()
    with es:
        e = es.enter_context

        xnT = e(nc.sbuf_tensor("xnT", [128, KC * B * RPC], BF16))
        bb = [e(nc.sbuf_tensor(f"bb{i}", [128, 8192], BF16)) for i in range(6)]
        aux_h = e(nc.sbuf_tensor("aux_h", [128, 3072], BF16))
        c_f32 = e(nc.sbuf_tensor("c_f32", [128, KC * RPC], F32))
        c_bfA = e(nc.sbuf_tensor("c_bfA", [128, KC * RPC], BF16))
        c_bfB = e(nc.sbuf_tensor("c_bfB", [128, KC * RPC], BF16))
        scr = e(nc.sbuf_tensor("scr", [128, 512], F32))
        scr2 = e(nc.sbuf_tensor("scr2", [128, 512], F32))
        gates = [e(nc.sbuf_tensor(f"gate{i}", [128, RPC], F32)) for i in range(3)]
        upd = e(nc.sbuf_tensor("upd", [128, RPC], F32))
        sq_h = e(nc.sbuf_tensor("sq_h", [128, RPC], BF16))
        tmp_h = e(nc.sbuf_tensor("tmp_h", [128, RPC], BF16))
        muB = e(nc.sbuf_tensor("muB", [128, RPC], BF16))
        rsB = e(nc.sbuf_tensor("rsB", [128, RPC], BF16))
        hT_b = e(nc.sbuf_tensor("hT_b", [128, RPC], BF16))
        xres = [e(nc.sbuf_tensor(f"xres{i}", [128, RPC], F32)) for i in range(2)]
        outs = [e(nc.sbuf_tensor(f"outs{i}", [128, RPC], F32)) for i in range(2)]
        stat = e(nc.sbuf_tensor("stat", [1, 2 * RPC], F32))
        uvbv_row = e(nc.sbuf_tensor("uvbv_row", [1, E], BF16))
        ones_col = e(nc.sbuf_tensor("ones_col", [128, 1], BF16))
        ones_row = e(nc.sbuf_tensor("ones_row", [1, 128], BF16))
        ones_row_f = e(nc.sbuf_tensor("ones_row_f", [1, 128], F32))
        ident_h = e(nc.sbuf_tensor("ident_h", [128, 128], BF16))
        sin_sb = e(nc.sbuf_tensor("sin_sb", [128, RPC], F32))
        cos_sb = e(nc.sbuf_tensor("cos_sb", [128, RPC], F32))
        sm = e(nc.sbuf_tensor("sm", [128, 96], F32))

        p_m = [e(nc.psum_tensor(f"p_m{i}", [128, RPC], F32)) for i in range(3)]
        p_v = [e(nc.psum_tensor(f"p_v{i}", [128, 512], F32)) for i in range(2)]
        p_base = e(nc.psum_tensor("p_base", [128, RPC], F32))
        p_h = e(nc.psum_tensor("p_h", [128, RPC], F32))
        p_stat = e(nc.psum_tensor("p_stat", [128, 512], F32))
        p_s = [p_stat[0:1, 0:RPC], p_stat[0:1, RPC:2 * RPC]]
        p_b = p_stat

        P = Prog(nc, es)
        psum_owner = {}

        def group_begin(key):
            if key in psum_owner:
                s, v = psum_owner[key]
                P.wait("tensor", s, v)

        smcol = {}
        _c = [0]
        def alloc_sm(name, ncols):
            smcol[name] = _c[0]
            _c[0] += ncols
        for name in ("ln_w", "ln_b", "o_b", "bf", "bi", "bc", "bo", "in_c"):
            alloc_sm(name, KC)
        alloc_sm("uvb_u", ET)
        alloc_sm("uvb_b", 1)
        alloc_sm("uvb_bs", 1)
        alloc_sm("gamma", 2)
        alloc_sm("beta", 2)
        alloc_sm("gammas", 2)
        alloc_sm("betas", 2)
        alloc_sm("g0p", 1)
        alloc_sm("b0p", 1)
        alloc_sm("g0ps", 1)
        alloc_sm("b0ps", 1)
        assert _c[0] <= 96

        def smc(name, i=0):
            return sm[:, smcol[name] + i: smcol[name] + i + 1]

        WG = {g: bb[i] for i, g in enumerate(("wf", "uf", "wi", "ui", "wc", "uc"))}
        kernelT = bb[0]
        kT_sb = bb[0][:, 4096:6144]
        qT_bufs = [bb[0][:, 6144:6400], bb[0][:, 6400:6656]]
        uT_bufs = [bb[1][:, 0:4096], bb[1][:, 4096:8192]]
        eT = bb[3][:, 4096:8192]
        biasT_sb = bb[2][:, 0:4096]
        uocT = bb[2][:, 4096:6144]
        xnnT = bb[2][:, 6144:8192]
        ow_strip = [bb[3][:, i * 2048:(i + 1) * 2048] for i in range(2)]
        woT_sb = bb[4]
        wu_strip = [bb[5][:, i * 1024:(i + 1) * 1024] for i in range(2)]
        wv_buf = [bb[5][:, 2048 + i * 512: 2048 + (i + 1) * 512] for i in range(2)]
        wb_buf = bb[5][:, 3072:4096]
        vstrip = [bb[5][:, 4096 + i * 2048: 4096 + (i + 1) * 2048] for i in range(2)]
        vout = [aux_h[:, i * 512:(i + 1) * 512] for i in range(2)]
        uo_strip = [aux_h[:, 1024 + i * 1024: 1024 + (i + 1) * 1024] for i in range(2)]
        x_bf = c_bfA
        uvbv_hlp = c_bfA
        cast_h = c_bfB

        import dataclasses as _dc

        def mkap(base_ap, offset, dims):
            return _dc.replace(base_ap, offset=offset,
                               ap=type(base_ap.ap)(dims))

        def dma(eng, sem, out, in_, slow=False):
            P.dsem(sem, eng)
            if slow:
                return P.emit(eng, lambda e, o=out, i=in_: e.dma_start(
                    out=o, in_=i, allow_slow_non_contiguous=True), inc=sem, amt=16)
            return P.emit(eng, lambda e, o=out, i=in_: e.dma_start(out=o, in_=i),
                          inc=sem, amt=16)

        for name in ("ln_w", "ln_b", "o_b", "bf", "bi", "bc", "bo", "in_c"):
            dma("sync", "LS", sm[:, smcol[name]:smcol[name] + KC],
                vec_in[name].rearrange("(t p) -> p t", p=128), slow=True)
        dma("sync", "LS", sm[:, smcol["uvb_u"]:smcol["uvb_u"] + ET],
            vec_in["uvb_u"].rearrange("(t p) -> p t", p=128), slow=True)
        dma("sync", "LS", sm[:, smcol["uvb_b"]:smcol["uvb_b"] + 1],
            vec_in["uvb_b"].rearrange("(t p) -> p t", p=128), slow=True)
        dma("sync", "LS", sm[:, smcol["uvb_bs"]:smcol["uvb_bs"] + 1],
            vec_in["uvb_bs"].rearrange("(t p) -> p t", p=128), slow=True)
        dma("sync", "LS", sm[:, smcol["gamma"]:smcol["gamma"] + 2],
            gamma_in.rearrange("h s -> s h"), slow=True)
        dma("sync", "LS", sm[:, smcol["beta"]:smcol["beta"] + 2],
            beta_in.rearrange("h s -> s h"), slow=True)
        dma("sync", "LS", sm[:, smcol["gammas"]:smcol["gammas"] + 2],
            gammas_in.rearrange("h s -> s h"), slow=True)
        dma("sync", "LS", sm[:, smcol["betas"]:smcol["betas"] + 2],
            betas_in.rearrange("h s -> s h"), slow=True)
        dma("sync", "LS", scr[:, :128], ident_in)
        dma("sync", "LS", sin_sb[:, :], sinT_in)
        dma("sync", "LS", cos_sb[:, :], cosT_in)

        P.wait("vector", "LS")
        P.emit("vector", lambda e: e.memset(ones_col[:, :], 1.0))
        P.emit("vector", lambda e: e.memset(ones_row[:, :], 1.0))
        P.emit("vector", lambda e: e.memset(ones_row_f[:, :], 1.0))
        P.emit("vector", lambda e: e.tensor_copy(ident_h[:, :], scr[:, :128]))
        for q in range(4):
            dma("sync", "L1", scr2[0:1, :512],
                uvb_v_in[q * 512:(q + 1) * 512].rearrange("(a m) -> a m", a=1))
            P.wait("vector", "L1")
            dvq = P.emit("vector", lambda e, q_=q: e.tensor_copy(
                uvbv_hlp[0:1, q_ * 512:(q_ + 1) * 512], scr2[0:1, :512]), inc="dv")
            P.wait("sync", "dv", dvq)
        P.wait("gpsimd", "dv")
        dma("gpsimd", "SU", uvbv_row[0:1, :], uvbv_hlp[0:1, 0:E])
        P.emit("vector", lambda e: e.tensor_scalar_mul(smc("g0p"), smc("gamma", 0), INV_SQRT_S))
        P.emit("vector", lambda e: e.tensor_scalar_mul(smc("b0p"), smc("beta", 0), INV_SQRT_S))
        P.emit("vector", lambda e: e.tensor_scalar_mul(smc("g0ps"), smc("gammas", 0), INV_SQRT_S))
        P.emit("vector", lambda e: e.tensor_scalar_mul(smc("b0ps"), smc("betas", 0), INV_SQRT_S),
               inc="dv")

        scr_guard = [0, 0]

        def cast_chunks(chunks):
            pass

        def cast_resident(src_ap, dst, n_out):
            i = 0
            for kc in range(KC):
                for off in range(0, n_out, 512):
                    wdt = min(512, n_out - off)
                    par = i % 2
                    buf = (scr, scr2)[par]
                    if scr_guard[par]:
                        P.wait("sync", "dv", scr_guard[par])
                    dma("sync", "L%d" % par, buf[:, :wdt],
                        src_ap[kc * 128:(kc + 1) * 128, off:off + wdt])
                    P.wait("vector", "L%d" % par)
                    dvv = P.emit("vector", lambda e, b=buf[:, :wdt],
                                 d=dst[:, kc * n_out + off: kc * n_out + off + wdt]:
                                 e.tensor_copy(d, b), inc="dv")
                    scr_guard[par] = dvv
                    i += 1

        cast_dram_guard = [0, 0]

        def cast_to_dram(src_ap, dst_dram, rows, cols):
            i = 0
            for r0 in range(0, rows, 128):
                for off in range(0, cols, 512):
                    wdt = min(512, cols - off)
                    par = i % 2
                    buf = (scr, scr2)[par]
                    if scr_guard[par]:
                        P.wait("sync", "dv", scr_guard[par])
                    dma("sync", "L%d" % par, buf[:, :wdt],
                        src_ap[r0:r0 + 128, off:off + wdt])
                    hbuf = cast_h[:, par * 1024: par * 1024 + wdt]
                    P.wait("vector", "L%d" % par)
                    if cast_dram_guard[par]:
                        P.wait("vector", "SC%d" % par)
                    dvv = P.emit("vector", lambda e, b=buf[:, :wdt], h=hbuf:
                                 e.tensor_copy(h, b), inc="dv")
                    scr_guard[par] = dvv
                    P.wait("gpsimd", "dv", dvv)
                    dma("gpsimd", "SC%d" % par, dst_dram.ap()[r0:r0 + 128, off:off + wdt], hbuf)
                    cast_dram_guard[par] = 1
                    i += 1

        for g in ("wf", "uf", "wi", "ui", "wc", "uc"):
            cast_resident(gateT[g], WG[g], H)
        cast_to_dram(gateT["uo"], uo_h, H, H)
        cast_to_dram(wuT, wu_h, H, E)
        cast_to_dram(wvT, wv_h, H, E)
        cast_to_dram(wbT, wb_h, H, S)
        cast_to_dram(wbTs, wbs_h, H, S)
        cast_to_dram(owT, ow_h, E, H)

        P.barrier()

        st_mu = stat[0:1, 0:RPC]
        st_b = stat[0:1, RPC:2 * RPC]

        def ln_stats(src_slices):
            P.wait("tensor", "dv")
            P.wait("scalar", "dv")
            if P.cnt.get("SK"):
                P.wait("scalar", "SK")
                P.wait("vector", "SK")
            for kc in range(KC):
                P.emit("tensor", lambda e, s=src_slices[kc], k=kc:
                       e.matmul(p_s[0], ones_col[:, :], s,
                                start=(k == 0), stop=(k == KC - 1)),
                       inc="pe")
            for kc in range(KC):
                sq = (sq_h, tmp_h)[kc % 2]
                if kc >= 2:
                    P.wait("scalar", "pe", pe_sq[kc - 2])
                av = P.emit("scalar", lambda e, s=src_slices[kc], q=sq:
                            e.activation(out=q[:, :], in_=s, func=AF.Square), inc="ac")
                P.wait("tensor", "ac", av)
                pv = P.emit("tensor", lambda e, k=kc, q=sq:
                            e.matmul(p_s[1], ones_col[:, :], q[:, :],
                                     start=(k == 0), stop=(k == KC - 1)),
                            inc="pe")
                if kc == 0:
                    pe_sq = {}
                pe_sq[kc] = pv
            P.wait("vector", "pe")
            P.emit("vector", lambda e: e.tensor_scalar_mul(st_mu, p_s[0], 1.0 / H))
            P.emit("vector", lambda e: e.tensor_scalar_mul(st_b, p_s[1], 1.0 / H))
            P.emit("vector", lambda e: e.tensor_mul(p_s[0], st_mu, st_mu))
            P.emit("vector", lambda e: e.tensor_sub(st_b, st_b, p_s[0]))
            P.emit("vector", lambda e: e.tensor_scalar_add(st_b, st_b, EPS), inc="dv")
            P.wait("scalar", "dv")
            P.emit("scalar", lambda e: e.activation(out=st_b, in_=st_b, func=AF.Sqrt),
                   inc="ac")
            P.wait("vector", "ac")
            P.emit("vector", lambda e: e.reciprocal(st_b, st_b), inc="dv")
            P.wait("tensor", "dv")
            P.emit("tensor", lambda e: e.matmul(p_b[:, 0:RPC], ones_row_f[:, :], st_mu,
                                                start=True, stop=True))
            P.emit("tensor", lambda e: e.matmul(p_b[:, RPC:2 * RPC], ones_row_f[:, :], st_b,
                                                start=True, stop=True), inc="pe")
            P.wait("vector", "pe")
            P.emit("vector", lambda e: e.tensor_copy(muB[:, :], p_b[:, 0:RPC]))
            P.emit("vector", lambda e: e.tensor_copy(rsB[:, :], p_b[:, RPC:2 * RPC]),
                   inc="dv")

        def xnT_sl(kc, t):
            off = kc * (B * RPC) + t * RPC
            return xnT[:, off: off + RPC]

        for t in range(B):
            P.wait("vector", "pe")
            for kc in range(KC):
                par = kc % 2
                buf = (scr, scr2)[par]
                if scr_guard[par]:
                    P.wait("sync", "dv", scr_guard[par])
                dma("sync", "L%d" % par, buf[:, :RPC],
                    x_own[t, kc * 128:(kc + 1) * 128, :])
                P.wait("vector", "L%d" % par)
                dvv = P.emit("vector", lambda e, b=buf[:, :RPC],
                             d=x_bf[:, kc * RPC:(kc + 1) * RPC]:
                             e.tensor_copy(d, b), inc="dv")
                scr_guard[par] = dvv
            xs = [x_bf[:, kc * RPC:(kc + 1) * RPC] for kc in range(KC)]
            ln_stats(xs)
            for kc in range(KC):
                P.emit("vector", lambda e, s=xs[kc]: e.tensor_sub(tmp_h[:, :], s, muB[:, :]))
                P.emit("vector", lambda e, d=xnT_sl(kc, t):
                       e.tensor_mul(d, tmp_h[:, :], rsB[:, :]), inc="dv")

        P.barrier()

        for mt in range(KC):
            P.emit("vector", lambda e, m=mt: e.tensor_scalar(
                out=c_f32[:, m * RPC:(m + 1) * RPC], in0=muB[:, :],
                scalar1=0.0, scalar2=smc("in_c", m), op0=ALU.mult, op1=ALU.add))
        for mt in range(KC):
            P.emit("vector", lambda e, m=mt: e.tensor_copy(
                c_bfA[:, m * RPC:(m + 1) * RPC], c_f32[:, m * RPC:(m + 1) * RPC]),
                inc="dv")

        gatespec = [("wf", "uf", "bf", AF.Sigmoid, 0),
                    ("wi", "ui", "bi", AF.Sigmoid, 1),
                    ("wc", "uc", "bc", AF.Tanh, 2)]
        dv_after_update = P.cnt["dv"]
        for t in range(B):
            cur = c_bfA if t % 2 == 0 else c_bfB
            nxt = c_bfB if t % 2 == 0 else c_bfA
            first_group = True
            for mt in range(KC):
                for (wg, ug, bg, func, gi) in gatespec:
                    key = "m%d" % ((mt * 3 + gi) % 3)
                    pm = p_m[(mt * 3 + gi) % 3]
                    group_begin(key)
                    for kc in range(KC):
                        P.emit("tensor", lambda e, p=pm, w=WG[wg], m=mt, k=kc, t_=t:
                               e.matmul(p[:, :],
                                        w[:, k * H + m * 128: k * H + m * 128 + 128],
                                        xnT_sl(k, t_), start=(k == 0), stop=False))
                    if first_group:
                        P.wait("tensor", "dv", dv_after_update)
                        first_group = False
                    for kc in range(KC):
                        P.emit("tensor", lambda e, p=pm, u=WG[ug], m=mt, k=kc, c=cur:
                               e.matmul(p[:, :],
                                        u[:, k * H + m * 128: k * H + m * 128 + 128],
                                        c[:, k * RPC:(k + 1) * RPC],
                                        start=False, stop=(k == KC - 1)),
                               inc="pe" if kc == KC - 1 else None)
                    pv = P.cnt["pe"]
                    P.wait("scalar", "pe", pv)
                    P.wait("scalar", "dv", dv_after_update if mt == 0 else dv_mt)
                    av = P.emit("scalar", lambda e, p=pm, f=func, b=bg, m=mt, g=gi:
                                e.activation(out=gates[g][:, :], in_=p[:, :], func=f,
                                             bias=smc(b, m)), inc="ac")
                    psum_owner[key] = ("ac", av)
                P.wait("vector", "ac")
                P.emit("vector", lambda e, m=mt: e.tensor_mul(
                    upd[:, :], gates[0][:, :], c_f32[:, m * RPC:(m + 1) * RPC]))
                P.emit("vector", lambda e: e.tensor_mul(
                    gates[2][:, :], gates[1][:, :], gates[2][:, :]))
                P.emit("vector", lambda e, m=mt: e.tensor_add(
                    c_f32[:, m * RPC:(m + 1) * RPC], upd[:, :], gates[2][:, :]))
                dv_mt = P.emit("vector", lambda e, m=mt, n=nxt: e.tensor_copy(
                    n[:, m * RPC:(m + 1) * RPC], c_f32[:, m * RPC:(m + 1) * RPC]),
                    inc="dv")
            dv_after_update = P.cnt["dv"]

        c_fin = c_bfA if B % 2 == 0 else c_bfB

        P.barrier()

        uoc_pe = {}
        for ot in range(KC):
            ustr = uo_strip[ot % 2]
            if ot >= 2:
                P.wait("sync", "pe", uoc_pe[ot - 2])
            dma("sync", "LR%d" % (ot % 2),
                ustr[:, :1024].rearrange("p (mc c) -> p mc c", mc=KC),
                mkap(uo_h.ap(), ot * 128, [[H, 128], [128 * H, KC], [1, 128]]))
            key = "base"
            pm = p_base
            group_begin(key)
            P.wait("tensor", "LR%d" % (ot % 2))
            for mc in range(KC):
                P.emit("tensor", lambda e, p=pm, s=ustr, m=mc:
                       e.matmul(p[:, :], s[:, m * 128:(m + 1) * 128],
                                c_fin[:, m * RPC:(m + 1) * RPC],
                                start=(m == 0), stop=(m == KC - 1)),
                       inc="pe" if mc == KC - 1 else None)
            pv = P.cnt["pe"]
            uoc_pe[ot] = pv
            P.wait("vector", "pe", pv)
            dvv = P.emit("vector", lambda e, o=ot, p=pm: e.tensor_copy(
                uocT[:, o * RPC:(o + 1) * RPC], p[:, :]), inc="dv")
            psum_owner[key] = ("dv", dvv)

        P.barrier()

        for jt in range(ET):
            par = jt % 2
            buf = (scr, scr2)[par]
            if scr_guard[par]:
                P.wait("sync", "dv", scr_guard[par])
            dma("sync", "L%d" % par, buf[:, :RPC],
                biasT_own[jt * 128:(jt + 1) * 128, :])
            P.wait("vector", "L%d" % par)
            dvv = P.emit("vector", lambda e, b=buf[:, :RPC], j=jt:
                         e.tensor_copy(biasT_sb[:, j * RPC:(j + 1) * RPC], b), inc="dv")
            scr_guard[par] = dvv
        for kc in range(KC):
            for hh in range(2):
                par = (kc * 2 + hh) % 2
                buf = (scr, scr2)[par]
                if scr_guard[par]:
                    P.wait("sync", "dv", scr_guard[par])
                dma("sync", "L%d" % par, buf[:, :],
                    woT[kc * 128:(kc + 1) * 128, hh * 512:(hh + 1) * 512])
                P.wait("vector", "L%d" % par)
                dvv = P.emit("vector", lambda e, b=buf[:, :], k=kc, h_=hh:
                             e.tensor_copy(woT_sb[:, k * H + h_ * 512:
                                                  k * H + h_ * 512 + 512], b),
                             inc="dv")
                scr_guard[par] = dvv

        P.barrier()

        def xnn_make(b):
            xs = [xnT_sl(kc, b) for kc in range(KC)]
            ln_stats(xs)
            P.wait("vector", "pe", produce_pe_end[0])
            for kc in range(KC):
                P.emit("vector", lambda e, s=xs[kc]: e.tensor_sub(tmp_h[:, :], s, muB[:, :]))
                P.emit("vector", lambda e: e.tensor_mul(sq_h[:, :], tmp_h[:, :], rsB[:, :]))
                P.emit("vector", lambda e, k=kc: e.tensor_scalar(
                    out=xnnT[:, k * RPC:(k + 1) * RPC], in0=sq_h[:, :],
                    scalar1=smc("ln_w", k), scalar2=smc("ln_b", k),
                    op0=ALU.mult, op1=ALU.add), inc="dv")

        base_sw = c_f32[:, 0:RPC]
        relu_t = c_f32[:, RPC:2 * RPC]
        rope_dv_last = [0]
        produce_pe_end = [0]
        cc_of = {}
        qk_pe_end = {}

        def rope_make(dst, gcol, bcol, gscol, bscol, a0, a1):
            P.emit("vector", lambda e, g=gcol, b_=bcol: e.tensor_scalar(
                out=upd[:, :], in0=gates[2][:, :], scalar1=g, scalar2=b_,
                op0=ALU.mult, op1=ALU.add))
            P.emit("vector", lambda e, g=gscol, b_=bscol: e.tensor_scalar(
                out=a0, in0=base_sw, scalar1=g, scalar2=b_,
                op0=ALU.mult, op1=ALU.add))
            P.emit("vector", lambda e: e.tensor_mul(a1, a0, sin_sb[:, :]))
            P.emit("vector", lambda e: e.tensor_mul(upd[:, :], upd[:, :], cos_sb[:, :]))
            return P.emit("vector", lambda e, d=dst:
                          e.tensor_add(d, upd[:, :], a1), inc="dv")

        vout_guard = [0, 0]
        out_guard = [0, 0]

        def produce(b):
            uT = uT_bufs[b % 2]
            qT_sb = qT_bufs[b % 2]
            xnn_make(b)
            dv_xnn = P.cnt["dv"]
            for et in range(ET):
                sb = wu_strip[et % 2]
                if et >= 2:
                    P.wait("sync", "pe", u_pe[et - 2])
                dma("sync", "LU%d" % (et % 2),
                    sb[:, :1024].rearrange("p (kc c) -> p kc c", kc=KC),
                    mkap(wu_h.ap(), et * 128, [[E, 128], [128 * E, KC], [1, 128]]))
                key = "pv%d" % (et % 2)
                pm = p_v[et % 2][:, 0:RPC]
                group_begin(key)
                P.wait("tensor", "LU%d" % (et % 2))
                if et == 0:
                    P.wait("tensor", "dv", dv_xnn)
                for kc in range(KC):
                    pv = P.emit("tensor", lambda e, p=pm, s=sb, k=kc:
                                e.matmul(p, s[:, k * 128:(k + 1) * 128],
                                         xnnT[:, k * RPC:(k + 1) * RPC],
                                         start=(k == 0), stop=(k == KC - 1)),
                                inc="pe")
                if et == 0:
                    u_pe = {}
                u_pe[et] = pv
                P.wait("scalar", "pe", pv)
                av = P.emit("scalar", lambda e, p=pm, t_=et, u_=uT: e.activation(
                    out=u_[:, t_ * RPC:(t_ + 1) * RPC], in_=p, func=AF.Silu,
                    bias=smc("uvb_u", t_)), inc="ac")
                psum_owner[key] = ("ac", av)
            P.wait("scalar", "dv", rope_dv_last[0])
            for swp, w_src, dst_sb, bias_name in (
                    (0, wb_h, gates[2][:, :], "uvb_b"),
                    (1, None, base_sw, "uvb_bs")):
                src = (wb_h if swp == 0 else wbs_h).ap()
                dma("sync", "LB",
                    wb_buf[:, :1024].rearrange("p (kc c) -> p kc c", kc=KC),
                    mkap(src, 0, [[S, 128], [128 * S, KC], [1, S]]))
                group_begin("base")
                P.wait("tensor", "LB")
                for kc in range(KC):
                    P.emit("tensor", lambda e, k=kc:
                           e.matmul(p_base[:, :], wb_buf[:, k * 128:(k + 1) * 128],
                                    xnnT[:, k * RPC:(k + 1) * RPC],
                                    start=(k == 0), stop=(k == KC - 1)),
                           inc="pe" if kc == KC - 1 else None)
                pv = P.cnt["pe"]
                P.wait("sync", "pe", pv)
                P.wait("scalar", "pe", pv)
                av = P.emit("scalar", lambda e, d=dst_sb, bn=bias_name: e.activation(
                    out=d, in_=p_base[:, :], func=AF.Silu,
                    bias=smc(bn)), inc="ac")
                psum_owner["base"] = ("ac", av)
            P.wait("vector", "ac", av)
            a0 = gates[0][:, :]
            a1 = gates[1][:, :]
            rope_make(qT_sb, smc("g0p"), smc("b0p"),
                      smc("g0ps"), smc("b0ps"), a0, a1)
            kv = rope_make(tmp_h[:, :], smc("gamma", 1), smc("beta", 1),
                           smc("gammas", 1), smc("betas", 1), a0, a1)
            rope_dv_last[0] = kv
            P.wait("gpsimd", "dv", kv)
            dma("gpsimd", "SK",
                ag_in[b].ap()[RPC:RPC + 16, :].rearrange("a (c i) -> (a c) i", i=RPC),
                tmp_h[:, :])
            for nch in range(4):
                for kc in range(KC):
                    par = (nch * KC + kc) % 2
                    wb2 = wv_buf[par]
                    if nch * KC + kc >= 2:
                        P.wait("sync", "pe", v_pe[nch * KC + kc - 2])
                    dma("sync", "LV%d" % par, wb2[:, :],
                        wv_h.ap()[kc * 128:(kc + 1) * 128, nch * 512:(nch + 1) * 512])
                    for rt in range(2):
                        key = "pv%d" % rt
                        if kc == 0:
                            group_begin(key)
                            P.emit("tensor", lambda e, p=p_v[rt], n=nch: e.matmul(
                                p[:, :], ones_row[:, :],
                                uvbv_row[0:1, n * 512:(n + 1) * 512],
                                start=True, stop=False))
                        P.wait("tensor", "LV%d" % par)
                        pv = P.emit("tensor", lambda e, p=p_v[rt], w=wb2, r=rt, k=kc:
                                    e.matmul(p[:, :],
                                             xnnT[:, k * RPC + r * 128: k * RPC + r * 128 + 128],
                                             w[:, :], start=False, stop=(k == KC - 1)),
                                    inc="pe")
                        if nch == 0 and kc == 0 and rt == 0:
                            v_pe = {}
                        v_pe[nch * KC + kc] = pv
                        if kc == KC - 1:
                            vo = vout[rt]
                            P.wait("scalar", "pe", pv)
                            if vout_guard[rt]:
                                P.wait("scalar", "SV%d" % rt)
                            av = P.emit("scalar", lambda e, p=p_v[rt], o=vo: e.activation(
                                out=o, in_=p[:, :], func=AF.Silu), inc="ac")
                            psum_owner[key] = ("ac", av)
                            P.wait("gpsimd", "ac", av)
                            dma("gpsimd", "SV%d" % rt,
                                ag_in[b].ap()[rt * 128:(rt + 1) * 128,
                                              nch * 512:(nch + 1) * 512], vo)
                            vout_guard[rt] = 1
            P.wait("gpsimd", "SV0")
            P.wait("gpsimd", "SV1")
            P.wait("gpsimd", "SK")
            P.emit("gpsimd", lambda e, b_=b: e.collective_compute(
                "AllGather", ALU.bypass, replica_groups=[list(range(NCORES))],
                ins=[ag_in[b_].ap().opt()], outs=[ag_out[b_].ap().opt()]), inc="cc")
            cc_of[b] = P.cnt["cc"]
            produce_pe_end[0] = P.cnt["pe"]

        def consume(b):
            uT = uT_bufs[b % 2]
            qT_sb = qT_bufs[b % 2]
            P.wait("gpsimd", "cc", cc_of[b])
            if b > 0:
                P.wait("gpsimd", "pe", qk_pe_end[b - 1])
            for r in range(NCORES):
                blk = ag_out[b].ap()[r * RPCB + RPC: r * RPCB + RPC + 16, :]
                dma("gpsimd", "LK", kT_sb[:, r * RPC:(r + 1) * RPC],
                    blk.rearrange("a (c i) -> (a c) i", i=RPC))
            for jt in range(ET):
                key = "m%d" % (jt % 2)
                pm = p_m[jt % 2]
                group_begin(key)
                if jt == 0:
                    P.wait("tensor", "LK")
                P.emit("tensor", lambda e, p=pm, j=jt: e.matmul(
                    p[:, :], ident_h[:, :], biasT_sb[:, j * RPC:(j + 1) * RPC],
                    start=True, stop=False))
                pv = P.emit("tensor", lambda e, p=pm, j=jt: e.matmul(
                    p[:, :], kT_sb[:, j * 128:(j + 1) * 128], qT_sb,
                    start=False, stop=True), inc="pe")
                P.wait("scalar", "pe", pv)
                if jt > 0:
                    P.wait("scalar", "dv", sq_dv)
                av = P.emit("scalar", lambda e, p=pm: e.activation(
                    out=relu_t, in_=p[:, :], func=AF.Relu), inc="ac")
                psum_owner[key] = ("ac", av)
                P.wait("vector", "ac", av)
                sq_dv = P.emit("vector", lambda e, j=jt: e.tensor_mul(
                    kernelT[:, j * RPC:(j + 1) * RPC], relu_t, relu_t), inc="dv")
            qk_pe_end[b] = P.cnt["pe"]
            dv_kernel = P.cnt["dv"]
            for et in range(ET):
                vs = vstrip[et % 2]
                if et >= 2:
                    P.wait("gpsimd", "pe", kv_pe[et - 2])
                for q in range(2):
                    dma("gpsimd", "LW%d" % (et % 2),
                        mkap(vs, vs.offset + q * 128,
                             [[8192, 128], [256, NCORES], [1, 128]]),
                        mkap(ag_out[b].ap(), et * 128 + q * 128 * E,
                             [[E, 128], [RPCB * E, NCORES], [1, 128]]))
                key = "m%d" % (et % 2)
                pm = p_m[et % 2]
                group_begin(key)
                P.wait("tensor", "LW%d" % (et % 2))
                if et == 0:
                    P.wait("tensor", "dv", dv_kernel)
                for jt in range(ET):
                    pv = P.emit("tensor", lambda e, p=pm, v_=vs, j=jt: e.matmul(
                        p[:, :], v_[:, j * 128:(j + 1) * 128],
                        kernelT[:, j * RPC:(j + 1) * RPC],
                        start=(j == 0), stop=(j == ET - 1)),
                        inc="pe" if jt == ET - 1 else None)
                if et == 0:
                    kv_pe = {}
                kv_pe[et] = pv
                P.wait("vector", "pe", pv)
                dvv = P.emit("vector", lambda e, p=pm, t_=et: e.tensor_mul(
                    eT[:, t_ * RPC:(t_ + 1) * RPC], p[:, :],
                    uT[:, t_ * RPC:(t_ + 1) * RPC]), inc="dv")
                psum_owner[key] = ("dv", dvv)
            dv_eT = P.cnt["dv"]
            for ft in range(KC):
                osb = ow_strip[ft % 2]
                if ft >= 2:
                    P.wait("sync", "pe", o_pe[ft - 2])
                dma("sync", "LO%d" % (ft % 2),
                    osb[:, :2048].rearrange("p (ec c) -> p ec c", ec=ET),
                    mkap(ow_h.ap(), ft * 128, [[H, 128], [128 * H, ET], [1, 128]]))
                group_begin("h")
                P.emit("tensor", lambda e, f=ft: e.matmul(
                    p_h[:, :], ident_h[:, :], uocT[:, f * RPC:(f + 1) * RPC],
                    start=True, stop=False))
                for kc in range(KC):
                    P.emit("tensor", lambda e, f=ft, k=kc, b_=b: e.matmul(
                        p_h[:, :], woT_sb[:, k * H + f * 128: k * H + f * 128 + 128],
                        xnT_sl(k, b_), start=False, stop=(k == KC - 1)),
                        inc="pe" if kc == KC - 1 else None)
                pv_h = P.cnt["pe"]
                P.wait("scalar", "pe", pv_h)
                if ft > 0:
                    P.wait("scalar", "dv", comb_dv)
                av = P.emit("scalar", lambda e, f=ft: e.activation(
                    out=hT_b[:, :], in_=p_h[:, :], func=AF.Tanh, bias=smc("bo", f)),
                    inc="ac")
                psum_owner["h"] = ("ac", av)
                key = "m%d" % (ft % 3)
                pm = p_m[ft % 3]
                group_begin(key)
                P.emit("tensor", lambda e, p=pm, f=ft, b_=b: e.matmul(
                    p[:, :], ident_h[:, :], xnT_sl(f, b_), start=True, stop=False))
                P.wait("tensor", "LO%d" % (ft % 2))
                if ft == 0:
                    P.wait("tensor", "dv", dv_eT)
                for ec in range(ET):
                    P.emit("tensor", lambda e, p=pm, s=osb, c_=ec: e.matmul(
                        p[:, :], s[:, c_ * 128:(c_ + 1) * 128],
                        eT[:, c_ * RPC:(c_ + 1) * RPC],
                        start=False, stop=(c_ == ET - 1)),
                        inc="pe" if ec == ET - 1 else None)
                pv_o = P.cnt["pe"]
                if ft == 0:
                    o_pe = {}
                o_pe[ft] = pv_o
                xr = xres[ft % 2]
                dma("sync", "LX%d" % (ft % 2), xr[:, :],
                    x_own[b, ft * 128:(ft + 1) * 128, :])
                ob = outs[ft % 2]
                P.wait("vector", "pe", pv_o)
                P.wait("vector", "ac", av)
                if out_guard[ft % 2]:
                    P.wait("vector", "SO%d" % (ft % 2))
                P.emit("vector", lambda e, p=pm, f=ft, o=ob: e.scalar_tensor_tensor(
                    out=o[:, :], in0=p[:, :], scalar=smc("o_b", f), in1=hT_b[:, :],
                    op0=ALU.add, op1=ALU.add), inc="dv")
                psum_owner[key] = ("dv", P.cnt["dv"])
                P.wait("vector", "LX%d" % (ft % 2))
                comb_dv = P.emit("vector", lambda e, o=ob, x=xr: e.tensor_add(
                    o[:, :], o[:, :], x[:, :]), inc="dv")
                P.wait("gpsimd", "dv", comb_dv)
                dma("gpsimd", "SO%d" % (ft % 2), y_own[b, ft * 128:(ft + 1) * 128, :],
                    ob[:, :])
                out_guard[ft % 2] = 1
                P.wait("sync", "dv", comb_dv)

        produce(0)
        for b in range(B):
            if b + 1 < B:
                produce(b + 1)
            consume(b)

        P.wait("gpsimd", "SO0")
        P.wait("gpsimd", "SO1")

        block = e(nc.Block())

        @block.sync
        def _(sync):
            for fn in P.ops["sync"]:
                fn(sync)

        @block.gpsimd
        def _(gpsimd):
            for fn in P.ops["gpsimd"]:
                fn(gpsimd)

        @block.tensor
        def _(tensor):
            for fn in P.ops["tensor"]:
                fn(tensor)

        @block.vector
        def _(vector):
            for fn in P.ops["vector"]:
                fn(vector)

        @block.scalar
        def _(scalar):
            for fn in P.ops["scalar"]:
                fn(scalar)

    return nc


_NC_CACHE = None


def kernel(**inputs):
    global _NC_CACHE
    x = np.asarray(inputs["L0_attn_x"], dtype=np.float32)
    sin_t, cos_t = _rope_tables()
    biasT = _bias_matrix(np.asarray(inputs["L0_attn_w"], np.float32))
    uv_w = np.asarray(inputs["L0_attn_uv_w"], np.float32)
    uv_b = np.asarray(inputs["L0_attn_uv_b"], np.float32)

    common = dict(
        ident_in=np.eye(128, dtype=np.float32),
        wuT=np.ascontiguousarray(uv_w[:E].T),
        wvT=np.ascontiguousarray(uv_w[E:2 * E].T),
        wbT=np.ascontiguousarray(uv_w[2 * E:].T),
        wbTs=np.ascontiguousarray(uv_w[2 * E:].T[:, _PERM]),
        owT=np.ascontiguousarray(np.asarray(inputs["L0_attn_o_w"], np.float32).T),
        woT=np.ascontiguousarray(np.asarray(inputs["L0_ffn_wo"], np.float32).T),
        wfT=np.ascontiguousarray(np.asarray(inputs["L0_ffn_wf"], np.float32).T),
        ufT=np.ascontiguousarray(np.asarray(inputs["L0_ffn_uf"], np.float32).T),
        wiT=np.ascontiguousarray(np.asarray(inputs["L0_ffn_wi"], np.float32).T),
        uiT=np.ascontiguousarray(np.asarray(inputs["L0_ffn_ui"], np.float32).T),
        wcT=np.ascontiguousarray(np.asarray(inputs["L0_ffn_wc"], np.float32).T),
        ucT=np.ascontiguousarray(np.asarray(inputs["L0_ffn_uc"], np.float32).T),
        uoT=np.ascontiguousarray(np.asarray(inputs["L0_ffn_uo"], np.float32).T),
        ln_w=np.asarray(inputs["L0_attn_ln_w"], np.float32),
        ln_b=np.asarray(inputs["L0_attn_ln_b"], np.float32),
        o_b=np.asarray(inputs["L0_attn_o_b"], np.float32),
        bf=np.asarray(inputs["L0_ffn_bf"], np.float32),
        bi=np.asarray(inputs["L0_ffn_bi"], np.float32),
        bc=np.asarray(inputs["L0_ffn_bc"], np.float32),
        bo=np.asarray(inputs["L0_ffn_bo"], np.float32),
        in_c=np.asarray(inputs["L0_ffn_in_c"], np.float32),
        uvb_u=np.ascontiguousarray(uv_b[:E]),
        uvb_v=np.ascontiguousarray(uv_b[E:2 * E]),
        uvb_b=np.ascontiguousarray(uv_b[2 * E:]),
        uvb_bs=np.ascontiguousarray(uv_b[2 * E:][_PERM]),
        gamma=np.asarray(inputs["L0_attn_gamma"], np.float32),
        beta=np.asarray(inputs["L0_attn_beta"], np.float32),
        gammas=np.ascontiguousarray(np.asarray(inputs["L0_attn_gamma"], np.float32)[:, _PERM]),
        betas=np.ascontiguousarray(np.asarray(inputs["L0_attn_beta"], np.float32)[:, _PERM]),
    )

    xT = np.ascontiguousarray(x.transpose(0, 2, 1))
    in_maps = []
    for c in range(NCORES):
        r0, r1 = c * RPC, (c + 1) * RPC
        m = dict(common)
        m["x_own"] = np.ascontiguousarray(xT[:, :, r0:r1])
        m["biasT_own"] = np.ascontiguousarray(biasT[:, r0:r1])
        st = sin_t[r0:r1].T
        ct = cos_t[r0:r1].T
        m["sinT"] = np.ascontiguousarray(np.concatenate([-st, st], axis=0))
        m["cosT"] = np.ascontiguousarray(np.concatenate([ct, ct], axis=0))
        in_maps.append(m)

    if _NC_CACHE is None:
        _NC_CACHE = build_program()
    res = run_bass_kernel_spmd(_NC_CACHE, in_maps, core_ids=list(range(NCORES)))
    shards = [res.results[c]["y_own"] for c in range(NCORES)]
    outT = np.concatenate(shards, axis=2)
    return np.ascontiguousarray(outT.transpose(0, 2, 1)).astype(np.float32)
